# version 07
# baseline (speedup 1.0000x reference)
import math
from contextlib import ExitStack

import numpy as np
import ml_dtypes

import concourse.bass as bass
import concourse.mybir as mybir
from concourse.ap import AP
from concourse.bass_utils import run_bass_kernel_spmd
from concourse.aot_env import allow_ici_all_to_all

F32 = mybir.dt.float32
BF16 = mybir.dt.bfloat16
I32 = mybir.dt.int32
AF = mybir.ActivationFunctionType
ALU = mybir.AluOpType

NCORE = 8
S = 8192
D = 4096
T = S // NCORE
HD = 128
C_SHIFT = {0: 20.0, 1: 5.0}
EPS = 1e-5

BF = ml_dtypes.bfloat16
TAPS = ()



class Sem:
    __slots__ = ("h", "n")

    def __init__(self, h):
        self.h = h
        self.n = 0


class OpTok:
    __slots__ = ("incs",)

    def __init__(self):
        self.incs = []

    def inc(self, sem, k=1):
        sem.n += k
        self.incs.append((sem, k))
        return sem.n


class EngRec:
    def __init__(self, prog, eng):
        self._p = prog
        self._e = eng

    def wait(self, sem, val):
        if val > 0:
            self._p.ops[self._e].append(("wait_ge", (sem.h, val), {}, OpTok()))

    def __getattr__(self, opname):
        p, e = self._p, self._e

        def call(*a, **k):
            tok = OpTok()
            p.ops[e].append((opname, a, k, tok))
            return tok
        return call


class P:

    def __init__(self, name):
        self.name = name
        self.nc = bass.Bass("TRN2", target_bir_lowering=False)
        self.ctx = ExitStack()
        self.ops = {e: [] for e in ("sync", "tensor", "vector", "scalar", "gpsimd")}
        self.t = EngRec(self, "tensor")
        self.v = EngRec(self, "vector")
        self.s = EngRec(self, "scalar")
        self.g = EngRec(self, "gpsimd")
        self.sy = EngRec(self, "sync")
        self.prefix = ""
        self._scopes = []

    def sem(self, name):
        return Sem(self.ctx.enter_context(self.nc.semaphore(self.prefix + name)))

    def _alloc(self, cm):
        h = cm.__enter__()
        if self._scopes:
            self._scopes[-1].append(cm)
        else:
            self.ctx.callback(lambda c=cm: None)
        return h

    def sb(self, name, shape, dt):
        return self._alloc(self.nc.sbuf_tensor(self.prefix + name, shape, dt))

    def psum(self, name, shape, dt):
        return self._alloc(self.nc.psum_tensor(self.prefix + name, shape, dt))

    def scoped_sb(self, name, shape, dt):
        cm = self.nc.sbuf_tensor(self.prefix + name, shape, dt)
        h = cm.__enter__()
        return h, (lambda: cm.__exit__(None, None, None))

    def scoped_psum(self, name, shape, dt):
        cm = self.nc.psum_tensor(self.prefix + name, shape, dt)
        h = cm.__enter__()
        return h, (lambda: cm.__exit__(None, None, None))

    def push_scope(self):
        self._scopes.append([])

    def pop_scope(self):
        for cm in reversed(self._scopes.pop()):
            cm.__exit__(None, None, None)

    def din(self, name, shape, dt):
        return self.nc.dram_tensor(name, shape, dt, kind="ExternalInput").ap()

    def dout(self, name, shape, dt):
        return self.nc.dram_tensor(name, shape, dt, kind="ExternalOutput").ap()

    def dram(self, name, shape, dt, addr_space="Local"):
        return self.nc.dram_tensor(name, shape, dt, kind="Internal",
                                   addr_space=addr_space).ap()

    def emit(self):
        blk = self.ctx.enter_context(self.nc.Block())

        def runner(eng_name):
            def run(eng):
                for (opname, a, k, tok) in self.ops[eng_name]:
                    inst = getattr(eng, opname)(*a, **k)
                    for (sem, kk) in tok.incs:
                        inst.then_inc(sem.h, kk)
            return run

        for eng_name, method in (("sync", blk.sync), ("tensor", blk.tensor),
                                 ("vector", blk.vector), ("scalar", blk.scalar),
                                 ("gpsimd", blk.gpsimd)):
            if self.ops[eng_name]:
                method(runner(eng_name))
        self.ctx.close()
        return self.nc


def pview(dram_ap, r0, nrows, c0, ncols, p=128):
    t = dram_ap.tensor
    C = t.shape[-1]
    off = dram_ap.offset + r0 * C + c0
    return AP(t, off, [[C, p], [p * C, nrows // p], [1, ncols]])


def gload(p, dst_sb, dram_ap, r0, nrows, c0, ncols, sem, dst_c0=0, tiles_per_dma=4,
          eng=None):
    eng = eng or p.g
    nt = nrows // 128
    for t0 in range(0, nt, tiles_per_dma):
        tcnt = min(tiles_per_dma, nt - t0)
        src = pview(dram_ap, r0 + 128 * t0, 128 * tcnt, c0, ncols)
        dst = dst_sb[:, dst_c0 + t0 * ncols: dst_c0 + (t0 + tcnt) * ncols]
        eng.dma_start(out=dst, in_=src).inc(sem, 16)
    return sem.n


def bcast_row(dram_ap, row, col0, ncols, nparts=128):
    t = dram_ap.tensor
    rowlen = t.shape[-1]
    off = dram_ap.offset + row * rowlen + col0
    return AP(t, off, [[0, nparts], [1, ncols]])


def rowsap(dram_ap, r0, nrows=128):
    t = dram_ap.tensor
    C = t.shape[-1]
    return AP(t, dram_ap.offset + r0 * C, [[C, nrows], [1, C]])



def emit_ln(p, layer, xT_in, xnT_sb, xnT_out, io, gates=()):
    p.prefix = f"ln{layer}_"
    ld = p.sem("ld")
    mm = p.sem("mm")
    ve = p.sem("ve")
    sc = p.sem("sc")
    stx = p.sem("stx")

    x_sb = [p.sb(f"x{i}", [128, T], F32) for i in range(3)]
    xq_sb = [p.sb(f"xq{i}", [128, T], F32) for i in range(2)]
    m_sb = p.sb("m", [1, T], F32)
    e2_sb = p.sb("e2", [1, T], F32)
    var_sb = p.sb("var", [1, T], F32)
    a_sb = p.sb("a", [1, T], F32)
    b_sb = p.sb("b", [1, T], F32)
    ab_sb = p.sb("ab", [128, T], F32)
    bb_sb = p.sb("bb", [128, T], F32)
    tmp_sb = p.sb("tmp", [128, T], F32)

    ps_sx, cl1 = p.scoped_psum("ps_sx", [1, T], F32)
    ps_sq, cl0 = p.scoped_psum("ps_sq", [1, T], F32)
    ps_a, cl2 = p.scoped_psum("ps_a", [128, T], F32)
    ps_b, cl3 = p.scoped_psum("ps_b", [128, T], F32)

    onescolf, onesrow = io["onescolf"], io["onesrow"]
    cld = io["constld"]

    mm_tok = [0] * 32
    ld_tok = [0] * 32
    for ct in range(32):
        b = ct % 3
        for (sm, val) in gates:
            if ct < 3:
                p.sy.wait(sm, val)
        if ct >= 3:
            p.sy.wait(mm, mm_tok[ct - 3])
        ld_tok[ct] = p.sy.dma_start(out=x_sb[b][:],
                                    in_=rowsap(xT_in, 128 * ct)).inc(ld, 16)
        b2 = ct % 2
        p.v.wait(ld, ld_tok[ct])
        if ct >= 2:
            p.v.wait(mm, mm_tok[ct - 2])
        ve_sq = p.v.tensor_mul(xq_sb[b2][:], x_sb[b][:], x_sb[b][:]).inc(ve, 1)
        p.t.wait(ld, ld_tok[ct])
        p.t.wait(ve, ve_sq)
        if ct == 0:
            p.t.wait(cld[0], cld[1])
        tok = None
        for th in range(2):
            p.t.matmul(ps_sx[0:1, th * 512:(th + 1) * 512], onescolf[:],
                       x_sb[b][:, th * 512:(th + 1) * 512],
                       start=(ct == 0), stop=(ct == 31))
            tok = p.t.matmul(ps_sq[0:1, th * 512:(th + 1) * 512], onescolf[:],
                             xq_sb[b2][:, th * 512:(th + 1) * 512],
                             start=(ct == 0), stop=(ct == 31))
        mm_tok[ct] = tok.inc(mm, 1)
    dmy = p.t.matmul(ps_a[0:1, 0:512], onescolf[:], x_sb[31 % 3][:, 0:512],
                     start=True, stop=True).inc(mm, 1)
    p.s.wait(mm, dmy)
    p.s.copy(m_sb[:], ps_sx[:])
    sc_ev = p.s.copy(e2_sb[:], ps_sq[:]).inc(sc, 1)

    p.v.wait(sc, sc_ev)
    p.v.tensor_scalar(out=m_sb[:], in0=m_sb[:], scalar1=1.0 / D, scalar2=None,
                      op0=ALU.mult)
    p.v.tensor_scalar(out=e2_sb[:], in0=e2_sb[:], scalar1=1.0 / D, scalar2=None,
                      op0=ALU.mult)
    p.v.tensor_mul(var_sb[:], m_sb[:], m_sb[:])
    p.v.tensor_sub(var_sb[:], e2_sb[:], var_sb[:])
    ve_v = p.v.tensor_scalar(out=var_sb[:], in0=var_sb[:], scalar1=EPS, scalar2=None,
                             op0=ALU.add).inc(ve, 1)
    p.s.wait(ve, ve_v)
    p.s.activation(out=var_sb[:], in_=var_sb[:], func=AF.Ln)
    sc_a = p.s.activation(out=a_sb[:], in_=var_sb[:], func=AF.Exp, scale=-0.5).inc(sc, 1)
    p.v.wait(sc, sc_a)
    ve_b = p.v.scalar_tensor_tensor(out=b_sb[:], in0=m_sb[:], scalar=-1.0, in1=a_sb[:],
                                    op0=ALU.mult, op1=ALU.mult).inc(ve, 1)
    p.t.wait(ve, ve_b)
    for th in range(2):
        p.t.matmul(ps_a[:, th * 512:(th + 1) * 512], onesrow[:],
                   a_sb[0:1, th * 512:(th + 1) * 512], start=True, stop=True)
        tok = p.t.matmul(ps_b[:, th * 512:(th + 1) * 512], onesrow[:],
                         b_sb[0:1, th * 512:(th + 1) * 512], start=True, stop=True)
    tok.inc(mm, 1)
    dmy2 = p.t.matmul(ps_sx[0:1, 0:512], onescolf[:], x_sb[31 % 3][:, 0:512],
                      start=True, stop=True).inc(mm, 1)
    p.v.wait(mm, dmy2)
    p.v.tensor_copy(ab_sb[:], ps_a[:])
    ve_bc = p.v.tensor_copy(bb_sb[:], ps_b[:]).inc(ve, 1)

    xn_done = [0] * 32
    st_tok = [0] * 8
    for ct in range(32):
        b = ct % 3
        if ct < 3:
            p.sy.wait(mm, mm_tok[31])
        else:
            p.sy.wait(ve, xn_done[ct - 3])
        ld2 = p.sy.dma_start(out=x_sb[b][:], in_=rowsap(xT_in, 128 * ct)).inc(ld, 16)
        p.v.wait(ld, ld2)
        if ct == 0:
            p.v.wait(ve, ve_bc)
        p.v.tensor_mul(tmp_sb[:], x_sb[b][:], ab_sb[:])
        xn_done[ct] = p.v.tensor_add(xnT_sb[:, ct * T:(ct + 1) * T], tmp_sb[:],
                                     bb_sb[:]).inc(ve, 1)
        if ct % 4 == 3:
            g4 = ct // 4
            p.sy.wait(ve, xn_done[ct])
            st_tok[g4] = p.sy.dma_start(
                out=pview(xnT_out, 128 * (g4 * 4), 512, 0, T),
                in_=xnT_sb[:, (g4 * 4) * T:(g4 * 4 + 4) * T]).inc(stx, 16)
    if io.get("dbg_xn") is not None:
        for g4 in range(8):
            p.sy.dma_start(out=pview(io["dbg_xn"], 128 * (g4 * 4), 512, 0, T),
                           in_=xnT_sb[:, (g4 * 4) * T:(g4 * 4 + 4) * T]).inc(stx, 16)
    cl3()
    cl2()
    cl0()
    cl1()
    return (stx, stx.n), (ve, xn_done[31])



def emit_ffn0(p, xnT_sb, xn_gate, gT_out, io, ldgates=()):
    p.prefix = "f0_"
    wrt, at, bt = io["wrt"], io["at"], io["bt"]
    br, rep, ones8, onesrow = io["br"], io["rep"], io["ones8"], io["onesrow"]

    ld = p.sem("ld")
    ldg = p.sem("ldg")
    mm = p.sem("mm")
    ve = p.sem("ve")
    sc = p.sem("sc")
    st = p.sem("st")

    wrt_bf = p.sb("wrt", [128, 32 * 8], BF16)
    at_bf = p.sb("at", [128, 32 * 128], BF16)
    bt_bf = p.sb("bt", [128, D], BF16)
    br_sb = p.sb("br", [8, 1], F32)
    rep_sb = p.sb("rep", [8, 128], F32)
    ones8_sb = p.sb("ones8", [8, 1], F32)
    elog = p.sb("elog", [8, T], F32)
    recip = p.sb("recip", [1, T], F32)
    rb_sb = p.sb("rb", [128, T], F32)
    scn = p.sb("scn", [128, T], F32)
    slow = p.sb("slow", [128, T], BF16)
    g_sb = [p.sb(f"g{i}", [128, T], F32) for i in range(2)]

    ps_log, cl1 = p.scoped_psum("pslog", [8, T], F32)
    ps_a, cl2 = p.scoped_psum("psa", [128, T], F32)
    ps_b, cl3 = p.scoped_psum("psb", [128, T], F32)
    ps_low, cl4 = p.scoped_psum("pslow", [128, T], F32)

    for (sm, val) in ldgates:
        p.sy.wait(sm, val)
        p.g.wait(sm, val)
    gload(p, wrt_bf, wrt, 0, D, 0, 8, ldg)
    gload(p, at_bf, at, 0, D, 0, 128, ldg)
    t3 = p.g.dma_start(out=bt_bf[:], in_=bt).inc(ldg, 16)
    p.sy.dma_start(out=br_sb[:], in_=br).inc(ld, 16)
    p.sy.dma_start(out=rep_sb[:], in_=rep).inc(ld, 16)
    t7 = p.sy.dma_start(out=ones8_sb[:], in_=ones8).inc(ld, 16)

    p.t.wait(ldg, t3)
    p.t.wait(ld, t7)
    p.t.wait(xn_gate[0], xn_gate[1])
    tok = None
    for th in range(2):
        for ct in range(32):
            tok = p.t.matmul(ps_log[:, th * 512:(th + 1) * 512],
                             wrt_bf[:, ct * 8:(ct + 1) * 8],
                             xnT_sb[:, ct * T + th * 512: ct * T + (th + 1) * 512],
                             start=(ct == 0), stop=(ct == 31))
    tok.inc(mm, 1)
    dmy1 = p.t.matmul(ps_a[0:1, 0:512], wrt_bf[:, 0:1], xnT_sb[:, 0:512],
                      start=True, stop=True).inc(mm, 1)
    p.s.wait(mm, dmy1)
    sc_exp = p.s.activation(out=elog[:], in_=ps_log[:], func=AF.Exp, bias=br_sb[:, 0:1]).inc(sc, 1)
    p.t.wait(sc, sc_exp)
    for th in range(2):
        tok = p.t.matmul(ps_log[0:1, th * 512:(th + 1) * 512], ones8_sb[:],
                         elog[:, th * 512:(th + 1) * 512], start=True, stop=True)
    tok.inc(mm, 1)
    for th in range(2):
        tok = p.t.matmul(ps_a[:, th * 512:(th + 1) * 512], rep_sb[:],
                         elog[:, th * 512:(th + 1) * 512], start=True, stop=True)
    mm_sc = tok.inc(mm, 1)
    p.v.wait(mm, mm_sc)
    ve_t = p.v.reciprocal(recip[:], ps_log[0:1, :]).inc(ve, 1)
    p.v.wait(ve, ve_t)
    ve_r16 = p.v.tensor_scalar(out=recip[:], in0=recip[:], scalar1=16.0, scalar2=None,
                               op0=ALU.mult).inc(ve, 1)
    p.t.wait(ve, ve_r16)
    for th in range(2):
        tok = p.t.matmul(ps_b[:, th * 512:(th + 1) * 512], onesrow[:],
                         recip[:, th * 512:(th + 1) * 512], start=True, stop=True)
    mm_rb = tok.inc(mm, 1)
    for th in range(2):
        for ct in range(32):
            tok = p.t.matmul(ps_low[:, th * 512:(th + 1) * 512],
                             at_bf[:, ct * 128:(ct + 1) * 128],
                             xnT_sb[:, ct * T + th * 512: ct * T + (th + 1) * 512],
                             start=(ct == 0), stop=(ct == 31))
    mm_low = tok.inc(mm, 1)
    dmy2 = p.t.matmul(ps_log[0:1, 0:512], wrt_bf[:, 0:1], xnT_sb[:, 0:512],
                      start=True, stop=True).inc(mm, 1)
    p.s.wait(mm, mm_low)
    sc_rb = p.s.copy(rb_sb[:], ps_b[:]).inc(sc, 1)
    p.v.wait(sc, sc_rb)
    ve_scn = p.v.tensor_mul(scn[:], ps_a[:], rb_sb[:]).inc(ve, 1)
    p.v.wait(mm, dmy2)
    p.v.wait(ve, ve_scn)
    ve_slow = p.v.tensor_mul(slow[:], ps_low[:], scn[:]).inc(ve, 1)
    if io.get("dbg") is not None:
        p.sy.wait(sc, sc_exp)
        p.sy.dma_start(out=io["dbg"]["elog"], in_=elog[:]).inc(st, 16)
        p.sy.wait(ve, ve_scn)
        p.sy.dma_start(out=io["dbg"]["scn"], in_=scn[:]).inc(st, 16)
        p.sy.wait(ve, ve_slow)
        p.sy.dma_start(out=io["dbg"]["slow"], in_=slow[:]).inc(st, 16)
    p.t.wait(ve, ve_slow)
    st_tok = [0, 0]
    mm_t = [0] * 33

    def epilogue(ot, chase):
        b = ot % 2
        pg = ps_a if b == 0 else ps_b
        p.s.wait(mm, chase)
        p.s.wait(st, st_tok[b])
        sc_t = p.s.activation(out=g_sb[b][:], in_=pg[:], func=AF.Silu).inc(sc, 1)
        p.sy.wait(sc, sc_t)
        st_tok[b] = p.sy.dma_start(out=gT_out[ot * 128:(ot + 1) * 128, :],
                                   in_=g_sb[b][:]).inc(st, 16)

    for ot in range(32):
        b = ot % 2
        pg = ps_a if b == 0 else ps_b
        if ot >= 2:
            p.t.wait(st, st_tok[b])
        for th in range(2):
            tok = p.t.matmul(pg[:, th * 512:(th + 1) * 512],
                             bt_bf[:, ot * 128:(ot + 1) * 128],
                             slow[:, th * 512:(th + 1) * 512], start=True, stop=True)
        mm_t[ot] = tok.inc(mm, 1)
        if ot >= 1:
            epilogue(ot - 1, mm_t[ot])
    dmy3 = p.t.matmul(ps_low[0:1, 0:512], bt_bf[:, 0:1], slow[:, 0:512],
                      start=True, stop=True).inc(mm, 1)
    epilogue(31, dmy3)
    cl4()
    cl3()
    cl2()
    cl1()
    return st, st.n


def emit_ffn1(p, xnT_sb, xn_gate, gT_out, io, ldgates=()):
    p.prefix = "f1_"
    qwrep, qzeros, scales = io["qwrep"], io["qzeros"], io["scales"]
    fbias, shiftv, e32 = io["fbias"], io["shiftv"], io["e32"]

    ld = p.sem("ld")
    ldg = p.sem("ldg")
    mm = p.sem("mm")
    ve = p.sem("ve")
    sc = p.sem("sc")
    st = p.sem("st")
    dq = p.sem("dq")

    qz_sb = p.sb("qz", [32, 512], I32)
    zr_sb = p.sb("zr", [32, D], I32)
    sc_sb = p.sb("scales", [32, D], F32)
    negsz = p.sb("negsz", [32, D], F32)
    fbias_sb = p.sb("fb", [128, 32], F32)
    shv_sb = p.sb("shv", [128, 1], I32)
    e32_sb = p.sb("e32", [128, 32 * 32], BF16)
    xnsum = p.sb("xnsum", [32, T], F32)
    qw_sb = [p.sb(f"qw{i}", [128, 512], I32) for i in range(2)]
    sbc_sb = [p.sb(f"sbc{i}", [128, 512], F32) for i in range(2)]
    wnib = [p.sb(f"wn{i}", [128, 512], I32) for i in range(2)]
    w_bf = [p.sb(f"w{i}", [128, 512], BF16) for i in range(2)]
    g_sb = [p.sb(f"g{i}", [128, T], F32) for i in range(2)]

    ps_cl = []
    ps_g = []
    for i in range(8):
        h, cl = p.scoped_psum(f"psg{i}", [128, 512], F32)
        ps_g.append(h)
        ps_cl.append(cl)

    for (sm, val) in ldgates:
        p.sy.wait(sm, val)
    t1 = p.sy.dma_start(out=qz_sb[:], in_=qzeros).inc(ld, 16)
    t2 = p.sy.dma_start(out=sc_sb[:], in_=scales).inc(ld, 16)
    t3 = p.sy.dma_start(out=fbias_sb[:], in_=fbias).inc(ld, 16)
    t4 = p.sy.dma_start(out=shv_sb[:], in_=shiftv).inc(ld, 16)
    t5 = p.sy.dma_start(out=e32_sb[:], in_=e32).inc(ld, 16)

    p.v.wait(ld, t2)
    ve_t = 0
    for i in range(8):
        zrview = zr_sb[:].rearrange("g (j i) -> g j i", i=8)[:, :, i]
        ve_t = p.v.tensor_scalar(out=zrview, in0=qz_sb[:], scalar1=4 * i, scalar2=15,
                                 op0=ALU.logical_shift_right, op1=ALU.bitwise_and).inc(ve, 1)
    p.v.wait(ve, ve_t)
    ve_negsz = p.v.scalar_tensor_tensor(out=negsz[:], in0=zr_sb[:], scalar=-1.0, in1=sc_sb[:],
                                        op0=ALU.mult, op1=ALU.mult).inc(ve, 1)
    p.t.wait(ld, t5)
    p.t.wait(xn_gate[0], xn_gate[1])
    tok = None
    for th in range(2):
        for t in range(32):
            tok = p.t.matmul(ps_g[th][0:32, :],
                             e32_sb[:, t * 32:(t + 1) * 32],
                             xnT_sb[:, t * T + th * 512: t * T + (th + 1) * 512],
                             start=(t == 0), stop=(t == 31))
    tok.inc(mm, 1)
    dmy = p.t.matmul(ps_g[2][0:1, :], e32_sb[:, 0:1], xnT_sb[:, 0:512],
                     start=True, stop=True).inc(mm, 1)
    p.s.wait(mm, dmy)
    p.s.copy(xnsum[:, 0:512], ps_g[0][0:32, :])
    p.s.copy(xnsum[:, 512:1024], ps_g[1][0:32, :])
    sc_xn = p.s.copy(g_sb[0][0:32, 0:512], xnsum[0:32, 0:512]).inc(sc, 1)

    st_of = {}
    dq_tok = {}
    ve_tok = {}
    og_st = 0
    idx = 0
    for og in range(8):
        of0 = og * 4
        for t in range(32):
            b = idx % 2
            if idx >= 2:
                p.sy.wait(dq, dq_tok[idx - 2])
            p.sy.dma_start(out=qw_sb[b][:],
                           in_=qwrep[128 * t:128 * (t + 1), 512 * og:512 * (og + 1)]).inc(ldg, 16)
            lt2 = p.sy.dma_start(out=sbc_sb[b][:],
                                 in_=bcast_row(scales, t, 512 * og, 512)).inc(ldg, 16)
            p.v.wait(ldg, lt2)
            if idx == 0:
                p.v.wait(ld, t4)
            if idx >= 2:
                p.v.wait(dq, dq_tok[idx - 2])
            ve_t = p.v.tensor_scalar(out=wnib[b][:], in0=qw_sb[b][:], scalar1=shv_sb[:],
                                     scalar2=15, op0=ALU.logical_shift_right,
                                     op1=ALU.bitwise_and).inc(ve, 1)
            p.v.wait(ve, ve_t)
            ve_tok[idx] = p.v.tensor_mul(w_bf[b][:], wnib[b][:], sbc_sb[b][:]).inc(ve, 1)
            p.t.wait(ve, ve_tok[idx])
            if t == 0:
                if og > 0:
                    p.t.wait(st, og_st)
                else:
                    p.t.wait(sc, sc_xn)
            tok = None
            for q in range(4):
                for th in range(2):
                    tok = p.t.matmul(ps_g[q * 2 + th][:],
                                     w_bf[b][:, 128 * q:128 * (q + 1)],
                                     xnT_sb[:, t * T + th * 512: t * T + (th + 1) * 512],
                                     start=(t == 0), stop=False)
            dq_tok[idx] = tok.inc(dq, 1)
            idx += 1
        p.t.wait(ve, ve_negsz)
        for q in range(4):
            for th in range(2):
                tok = p.t.matmul(ps_g[q * 2 + th][:],
                                 negsz[:, 128 * (of0 + q):128 * (of0 + q + 1)],
                                 xnsum[:, th * 512:(th + 1) * 512],
                                 start=False, stop=True)
        mm_t = tok.inc(mm, 1)
        p.s.wait(mm, mm_t)
        if og == 0:
            p.s.wait(ld, t3)
        for q in range(4):
            of = of0 + q
            qb = (og * 4 + q) % 2
            p.s.wait(st, st_of.get(qb, 0))
            sc_t = None
            for th in range(2):
                sc_t = p.s.activation(out=g_sb[qb][:, th * 512:(th + 1) * 512],
                                      in_=ps_g[q * 2 + th][:], func=AF.Silu,
                                      bias=fbias_sb[:, of:of + 1]).inc(sc, 1)
            p.sy.wait(sc, sc_t)
            st_of[qb] = p.sy.dma_start(out=gT_out[of * 128:(of + 1) * 128, :],
                                       in_=g_sb[qb][:]).inc(st, 16)
        og_st = st.n
    for cl in reversed(ps_cl):
        cl()
    return st, st.n



def emit_qkv(p, layer, xnTf, outs, io, wgates=(), ccgate=None):
    p.prefix = f"q{layer}_"
    p.push_scope()
    wcols = {0: 1536, 1: 768}[layer]
    nob = {0: 12, 1: 6}[layer]
    w_in = io["w"]
    cq, sq, ck, sk = io["cq"], io["sq"], io["ck"], io["sk"]

    ld = p.sem("ld")
    ldw = p.sem("ldw")
    xl = p.sem("xl")
    mm = p.sem("mm")
    ve = p.sem("ve")
    sc = p.sem("sc")
    st = p.sem("st")
    po = p.sem("po")

    w_sb = p.sb("w", [128, 32 * wcols], BF16)
    xn_sb = [p.sb(f"xn{i}", [128, 32 * 512], BF16) for i in range(2)]
    rope_sb = [p.sb(f"rope{i}", [128, 512], F32) for i in range(4)]
    ev_sb = [p.sb(f"ev{i}", [128, 512], F32) for i in range(2)]
    esw_sb = [p.sb(f"esw{i}", [128, 512], F32) for i in range(2)]
    t1_sb = [p.sb(f"t1{i}", [128, 512], F32) for i in range(2)]
    t2_sb = [p.sb(f"t2{i}", [128, 512], F32) for i in range(2)]
    out_sb = [p.sb(f"out{i}", [128, 512], BF16) for i in range(4)]
    if layer == 1:
        bq_sb = p.sb("bq", [128, 6], F32)
    ps_cl = []
    ps = []
    for i in range(6):
        h, cl = p.scoped_psum(f"ps{i}", [128, 512], F32)
        ps.append(h)
        ps_cl.append(cl)
    ps_junk, cl_junk = p.scoped_psum("psjunk", [1, 512], F32)
    ps_cl.append(cl_junk)
    nglobal = 16 * nob

    for (sm, val) in wgates:
        p.g.wait(sm, val)
        p.sy.wait(sm, val)
    if ccgate is not None:
        p.sy.wait(ccgate[0], ccgate[1])
    wt = gload(p, w_sb, w_in, 0, D, 0, wcols, ldw)
    if layer == 1:
        tb = p.sy.dma_start(out=bq_sb[:], in_=io["bqkv"]).inc(ld, 16)
    p.t.wait(ldw, wt)
    if layer == 1:
        p.t.wait(ld, tb)

    if layer == 0:
        kinds = ["q"] * 4 + ["k"] * 4 + ["v"] * 4
        head_of_ob = [0, 1, 2, 3, 0, 1, 2, 3, 0, 1, 2, 3]
    else:
        kinds = ["q"] * 4 + ["k", "v"]
        head_of_ob = [0, 1, 2, 3, 0, 0]

    mm_end = [0] * 16
    sc_tok = {}
    st_tok = [0] * 4
    ve_tok = {}
    po_last = 0
    ve4_tok = [0, 0]
    rt_toks = [0] * 16
    gidx = 0
    for tc in range(16):
        xb = tc % 2
        r, th = tc // 2, tc % 2
        if tc < 2:
            if ccgate is not None:
                p.g.wait(ccgate[0], ccgate[1])
        else:
            p.g.wait(mm, mm_end[tc - 2])
        xt = gload(p, xn_sb[xb], xnTf, r * D, D, th * 512, 512, xl)
        if tc >= 1:
            p.sy.wait(ve, ve_tok.get(tc - 1, 0))
        rt_t = None
        for i, tabap in enumerate((cq, sq, ck, sk)):
            p.sy.dma_start(out=rope_sb[i][0:64, :],
                           in_=tabap[:, tc * 512:(tc + 1) * 512]).inc(ld, 16)
            rt_t = p.sy.dma_start(out=rope_sb[i][64:128, :],
                                  in_=tabap[:, tc * 512:(tc + 1) * 512]).inc(ld, 16)
        rt_toks[tc] = rt_t
        p.t.wait(xl, xt)
        for ob in range(nob):
            pb = ps[ob % 6]
            if gidx >= 5:
                p.t.wait(sc, sc_tok[gidx - 5])
            tok = None
            for ct in range(32):
                tok = p.t.matmul(pb[:],
                                 w_sb[:, ct * wcols + ob * 128: ct * wcols + (ob + 1) * 128],
                                 xn_sb[xb][:, ct * 512:(ct + 1) * 512],
                                 start=(ct == 0), stop=(ct == 31))
            mm_t = tok.inc(mm, 1)
            chase = mm_t + 1 if gidx < nglobal - 1 else nglobal + 1
            kind = kinds[ob]
            h = head_of_ob[ob]
            obuf = ob % 4 if layer == 0 else (ob if ob < 4 else ob - 2)
            obuf = obuf % 4
            if kind == "v":
                ob_buf = out_sb[obuf]
                p.s.wait(mm, chase)
                p.s.wait(st, st_tok[obuf])
                if layer == 1:
                    sc_t = p.s.add(ob_buf[:], pb[:], bq_sb[:, 5:6]).inc(sc, 1)
                else:
                    sc_t = p.s.copy(ob_buf[:], pb[:]).inc(sc, 1)
                sc_tok[gidx] = sc_t
                p.sy.wait(sc, sc_t)
                st_tok[obuf] = p.sy.dma_start(
                    out=outs["vT"][h * 128:(h + 1) * 128, tc * 512:(tc + 1) * 512],
                    in_=ob_buf[:]).inc(st, 16)
            else:
                e2 = ob % 2
                eb = ev_sb[e2]
                p.s.wait(mm, chase)
                p.s.wait(po, po_last)
                p.s.wait(ve, ve4_tok[e2])
                if layer == 1:
                    bias = bq_sb[:, ob:ob + 1] if kind == "q" else bq_sb[:, 4:5]
                    sc_t = p.s.add(eb[:], pb[:], bias).inc(sc, 1)
                else:
                    sc_t = p.s.copy(eb[:], pb[:]).inc(sc, 1)
                sc_tok[gidx] = sc_t
                c_t = rope_sb[0] if kind == "q" else rope_sb[2]
                s_t = rope_sb[1] if kind == "q" else rope_sb[3]
                tb1 = t1_sb[e2]
                tb2 = t2_sb[e2]
                esw = esw_sb[e2]
                ob_buf = out_sb[obuf]
                p.g.wait(sc, sc_t)
                p.g.wait(ve, ve4_tok[e2])
                p.g.tensor_copy(esw[0:64, :], eb[64:128, :])
                gsw = p.g.tensor_copy(esw[64:128, :], eb[0:64, :]).inc(po, 1)
                po_last = gsw
                p.v.wait(sc, sc_t)
                p.v.wait(po, gsw)
                p.v.wait(ld, rt_toks[tc])
                p.v.wait(st, st_tok[obuf])
                p.v.tensor_mul(tb1[:], eb[:], c_t[:])
                p.v.tensor_mul(tb2[:], esw[:], s_t[:]).inc(ve, 1)
                p.v.tensor_sub(ob_buf[0:64, :], tb1[0:64, :], tb2[0:64, :])
                ve_4 = p.v.tensor_add(ob_buf[64:128, :], tb1[64:128, :], tb2[64:128, :]).inc(ve, 1)
                ve4_tok[e2] = ve_4
                ve_tok[tc] = ve_4
                p.sy.wait(ve, ve_4)
                dst = outs["qT"] if kind == "q" else outs["kT"]
                st_tok[obuf] = p.sy.dma_start(
                    out=dst[h * 128:(h + 1) * 128, tc * 512:(tc + 1) * 512],
                    in_=ob_buf[:]).inc(st, 16)
            gidx += 1
        mm_end[tc] = mm.n
    p.t.matmul(ps_junk[0:1, :], w_sb[:, 0:1], xn_sb[1][:, 0:512],
               start=True, stop=True).inc(mm, 1)
    for cl in reversed(ps_cl):
        cl()
    p.pop_scope()
    return st, st.n



def attn_pieces(layer, qc):
    out = []
    q_lo_t, q_hi_t = 4 * qc, 4 * qc + 3
    if layer == 0:
        for kt in range(0, q_hi_t + 1):
            col_lo = max(0, 128 * (kt - q_lo_t))
            diag = col_lo if kt >= q_lo_t else None
            out.append((kt, "full", col_lo, 512, diag))
    else:
        for kt in range(0, q_hi_t + 1):
            lo_t = max(q_lo_t, kt)
            hi_t = min(q_hi_t, kt + 7)
            if lo_t <= hi_t:
                col_lo = 128 * (lo_t - q_lo_t)
                col_hi = 128 * (hi_t - q_lo_t) + 128
                diag = 128 * (kt - q_lo_t) if kt >= q_lo_t else None
                out.append((kt, "full", col_lo, col_hi, diag))
            if kt % 4 == 3:
                lo_t = max(q_lo_t, kt + 8)
                if lo_t <= q_hi_t:
                    out.append((kt, "upper", 128 * (lo_t - q_lo_t), 512, None))
            else:
                i = kt + 8
                if q_lo_t <= i <= q_hi_t:
                    c0 = 128 * (i - q_lo_t)
                    out.append((kt, "upper", c0, c0 + 64, None))
    return out


def emit_attn(p, layer, io, gates=()):
    p.prefix = f"a{layer}_"
    p.push_scope()
    qT, kT, vT, ctxa = io["qT"], io["kT"], io["vT"], io["ctxa"]

    ld = p.sem("ld")
    ldh = p.sem("ldh")
    mm = p.sem("mm")
    ve = p.sem("ve")
    sc = p.sem("sc")
    st = p.sem("st")
    pv = p.sem("pv")

    ident_sb, tri_sb = io["ident_sb"], io["tri_sb"]
    onesc_sb, zeroc_sb = io["onescol_sb"], io["zerob_sb"]
    onesr_sb = io["onesrowf_sb"]
    csh_sb = io["csh0_sb"] if layer == 0 else io["csh1_sb"]
    cld = io["constld"]

    qh = [p.sb(f"q{i}", [128, S], BF16) for i in range(2)]
    kh = [p.sb(f"k{i}", [128, S], BF16) for i in range(2)]
    vh = [p.sb(f"v{i}", [128, S], BF16) for i in range(2)]
    vtm = [p.sb(f"vtm{i}", [128, S], BF16) for i in range(2)]
    pt = [p.sb(f"pt{i}", [128, 512], BF16) for i in range(3)]
    lrow = [p.sb(f"lrow{i}", [1, 512], F32) for i in range(2)]
    rb_sb = [p.sb(f"rb{i}", [128, 512], F32) for i in range(2)]
    ctx_sb = p.sb("ctx", [128, S], BF16)

    ps_cl = []

    def spsum(name, shape, dt):
        h, cl = p.scoped_psum(name, shape, dt)
        ps_cl.append(cl)
        return h

    ps_s = [spsum(f"pss{i}", [128, 512], F32) for i in range(3)]
    ps_o = [spsum(f"pso{i}", [128, 512], F32) for i in range(2)]
    ps_l = spsum("psl", [33, 512], F32)
    ps_rb = spsum("psrb", [128, 512], F32)
    ps_tr = spsum("pstr", [128, 512], BF16)

    ldh_tok = [0] * 4
    pv_end = [0] * 4
    vtm_ve = [0] * 4
    st_end = [0] * 4
    exp_tok = {}
    pvp_tok = {}
    veo_tok = {}
    scrb_tok = {}
    pieces_sbi = 0

    for (sm, val) in gates:
        p.g.wait(sm, val)

    for j in range(4):
        hb = j % 2
        kv = j if layer == 0 else 0
        if j >= 2:
            p.g.wait(pv, pv_end[j - 2])
        l1 = p.g.dma_start(out=qh[hb][:], in_=qT[128 * j:128 * (j + 1), :]).inc(ldh, 16)
        l2 = p.g.dma_start(out=kh[hb][:], in_=kT[128 * kv:128 * (kv + 1), :]).inc(ldh, 16)
        l3 = p.g.dma_start(out=vh[hb][:], in_=vT[128 * kv:128 * (kv + 1), :]).inc(ldh, 16)
        ldh_tok[j] = l3
        p.t.wait(ldh, l3)
        if j == 0:
            p.t.wait(cld[0], cld[1])
        if j >= 2:
            p.t.wait(pv, pv_end[j - 2])
        ev_pr = 0
        for grp in range(16):
            p.t.wait(ve, ev_pr)
            tok = None
            for q in range(4):
                ktile = grp * 4 + q
                tok = p.t.transpose(ps_tr[:, 128 * q:128 * (q + 1)],
                                    vh[hb][:, 128 * ktile:128 * (ktile + 1)], ident_sb[:])
            mm_t = tok.inc(mm, 1)
            p.v.wait(mm, mm_t)
            ev_pr = p.v.tensor_copy(vtm[hb][:, 512 * grp:512 * (grp + 1)], ps_tr[:]).inc(ve, 1)
        vtm_ve[j] = ev_pr
        p.t.wait(ve, ev_pr)
        p.t.wait(ldh, l1)

        for qc in range(16):
            pp = qc % 2
            pieces = attn_pieces(layer, qc)
            if (j, qc) != (0, 0):
                prev = veo_tok.get((j, qc - 2)) if qc >= 2 else veo_tok.get((j - 1, 14 + qc))
                if prev:
                    p.t.wait(ve, prev + 1)
            p.t.matmul(ps_o[pp][:, 0:512], zeroc_sb[:], kh[hb][:, 0:512],
                       start=True, stop=False)
            p.t.matmul(ps_l[32 * pp:32 * pp + 1, 0:512], zeroc_sb[:, 0:1],
                       kh[hb][:, 0:512], start=True, stop=False)
            for (kt, kpart, c0, c1_, diag) in pieces:
                sb = pieces_sbi % 3
                pieces_sbi += 1
                if kpart == "full":
                    lhs = kh[hb][:, 128 * kt:128 * (kt + 1)]
                    prow = slice(0, 128)
                else:
                    lhs = kh[hb][:, 128 * kt + 64:128 * (kt + 1)]
                    prow = slice(64, 128)
                if pieces_sbi > 2:
                    p.t.wait(sc, exp_tok[pieces_sbi - 1 - 2])
                s_t = p.t.matmul(ps_s[sb][prow, c0:c1_], lhs,
                                 qh[hb][:, 512 * qc + c0:512 * qc + c1_],
                                 start=True, stop=True).inc(mm, 1)
                p.s.wait(mm, s_t)
                if pieces_sbi > 3:
                    p.s.wait(pv, pvp_tok[pieces_sbi - 1 - 3])
                e_t = p.s.activation(out=pt[sb][prow, c0:c1_], in_=ps_s[sb][prow, c0:c1_],
                                     func=AF.Exp, bias=csh_sb[prow, 0:1]).inc(sc, 1)
                exp_tok[pieces_sbi - 1] = e_t
                m_t = None
                if diag is not None:
                    p.v.wait(sc, e_t)
                    m_t = p.v.tensor_mul(pt[sb][:, diag:diag + 128],
                                         pt[sb][:, diag:diag + 128], tri_sb[:]).inc(ve, 1)
                if m_t is not None:
                    p.t.wait(ve, m_t)
                else:
                    p.t.wait(sc, e_t)
                vlhs = vtm[hb][prow, 128 * kt:128 * (kt + 1)]
                p.t.matmul(ps_o[pp][:, c0:c1_], vlhs, pt[sb][prow, c0:c1_],
                           start=False, stop=False)
                pvp = p.t.matmul(ps_l[32 * pp:32 * pp + 1, c0:c1_], onesc_sb[prow, :],
                                 pt[sb][prow, c0:c1_],
                                 start=False, stop=False).inc(pv, 1)
                pvp_tok[pieces_sbi - 1] = pvp
            mm_all = pv.n
            p.v.wait(pv, mm_all)
            ve_l = p.v.reciprocal(lrow[pp][:], ps_l[32 * pp:32 * pp + 1, :]).inc(ve, 1)
            p.t.wait(ve, ve_l)
            prev_rb = scrb_tok.get((j, qc - 1)) or (scrb_tok.get((j - 1, 15)) if qc == 0 else None)
            if prev_rb:
                p.t.wait(sc, prev_rb + 1)
            r_t = p.t.matmul(ps_rb[:], onesr_sb[:], lrow[pp][:],
                             start=True, stop=True).inc(mm, 1)
            p.s.wait(mm, r_t)
            prev_ctx = veo_tok.get((j, qc - 2)) if qc >= 2 else veo_tok.get((j - 1, 14 + qc))
            if prev_ctx:
                p.s.wait(ve, prev_ctx + 1)
            sc_t = p.s.copy(rb_sb[pp][:], ps_rb[:]).inc(sc, 1)
            scrb_tok[(j, qc)] = sc_t
            p.v.wait(sc, sc_t)
            if qc == 0 and j > 0:
                p.v.wait(st, st_end[j - 1])
            veo_tok[(j, qc)] = p.v.tensor_mul(ctx_sb[:, 512 * qc:512 * (qc + 1)],
                                              ps_o[pp][:], rb_sb[pp][:]).inc(ve, 1)
        pv_end[j] = pv.n
        p.sy.wait(ve, veo_tok[(j, 15)])
        for r in range(NCORE):
            p.sy.dma_start(out=ctxa[512 * r + 128 * j: 512 * r + 128 * (j + 1), :],
                           in_=ctx_sb[:, T * r:T * (r + 1)]).inc(st, 16)
        st_end[j] = st.n
    p.sy.wait(st, st.n)
    for cl in reversed(ps_cl):
        cl()
    p.pop_scope()
    return st, st.n



def emit_oproj_ring(p, layer, wot, qst_gate, n_ungated=3):
    p.prefix = f"o{layer}_"
    bufs, closers = [], []
    for i in range(3):
        h, cl = p.scoped_sb(f"wo{i}", [128, D], BF16)
        bufs.append(h)
        closers.append(cl)
    ring = {
        "bufs": bufs,
        "closers": closers,
        "wl": p.sem("wl"),
        "op_mm": p.sem("op_mm"),
        "wot": wot,
        "qst_gate": qst_gate,
        "n_ungated": n_ungated,
    }
    return ring


def emit_ring_loads(p, layer, ring, first, last):
    p.prefix = f"o{layer}_"
    for ot in range(first, last):
        if ot < ring["n_ungated"]:
            sm, val = ring["qst_gate"]
            p.g.wait(sm, val)
        else:
            p.g.wait(ring["op_mm"], ot - 2)
        p.g.dma_start(out=ring["bufs"][ot % 3][:],
                      in_=rowsap(ring["wot"], 128 * ot)).inc(ring["wl"], 16)


def emit_oproj(p, layer, io, ring, ccgate):
    p.prefix = f"o{layer}_"
    p.push_scope()
    ctx_sl, obias = io["ctx_sl"], io["obias"]
    gT, xT_in, xoT = io["gT"], io["x"], io["xo"]

    ld = p.sem("ld")
    gx = p.sem("gx")
    mm = ring["op_mm"]
    ve = p.sem("ve")
    sto = p.sem("sto")

    ctx_sb = p.sb("ctx", [128, 32 * T], BF16)
    ob_sb = p.sb("ob", [128, 32], F32)
    g_sb = [p.sb(f"g{i}", [128, T], F32) for i in range(3)]
    x_sb = [p.sb(f"x{i}", [128, T], F32) for i in range(3)]
    tmp_sb = [p.sb(f"tmp{i}", [128, T], F32) for i in range(2)]
    xo_sb = [p.sb(f"xo{i}", [128, T], F32) for i in range(2)]

    ps_cl = []
    ps_s = []
    for i in range(3):
        h, cl = p.scoped_psum(f"pss{i}", [128, T], F32)
        ps_s.append(h)
        ps_cl.append(cl)
    ps_junk, cl_junk = p.scoped_psum("psjunk", [1, 512], F32)
    ps_cl.append(cl_junk)

    wl = ring["wl"]

    p.sy.wait(ccgate[0], ccgate[1])
    t0 = None
    for ct0 in range(0, 32, 4):
        t0 = p.sy.dma_start(out=ctx_sb[:, ct0 * T:(ct0 + 4) * T],
                            in_=pview(ctx_sl, 128 * ct0, 512, 0, T)).inc(ld, 16)
    t2 = p.sy.dma_start(out=ob_sb[:], in_=obias).inc(ld, 16)

    gstg = io.get("g_gate")
    gx_tok = [0] * 3
    ve_tok = [0] * 32
    st_tok = [0, 0]

    def load_gx(ot):
        b = ot % 3
        if ot < 3 and gstg is not None:
            p.sy.wait(gstg[0], gstg[1])
        gx_tok[b] = p.sy.dma_start(out=g_sb[b][:], in_=rowsap(gT, 128 * ot)).inc(gx, 16)
        gx_tok[b] = p.sy.dma_start(out=x_sb[b][:], in_=rowsap(xT_in, 128 * ot)).inc(gx, 16)

    load_gx(0)
    load_gx(1)
    load_gx(2)
    for ot in range(32):
        b = ot % 3
        b2 = ot % 2
        p.t.wait(wl, 16 * (ot + 1))
        if ot == 0:
            p.t.wait(ld, t0)
        if ot >= 3:
            p.t.wait(ve, ve_tok[ot - 2])
        tok = None
        for ct in range(32):
            for th in range(2):
                tok = p.t.matmul(ps_s[b][:, 512 * th:512 * (th + 1)],
                                 ring["bufs"][ot % 3][:, ct * 128:(ct + 1) * 128],
                                 ctx_sb[:, ct * T + 512 * th: ct * T + 512 * (th + 1)],
                                 start=(ct == 0), stop=(ct == 31))
        mm_t = tok.inc(mm, 1)
        p.v.wait(mm, mm_t + 1 if ot < 31 else 33)
        p.v.wait(gx, gx_tok[b])
        if ot == 0:
            p.v.wait(ld, t2)
        p.v.wait(sto, st_tok[b2])
        p.v.scalar_tensor_tensor(out=tmp_sb[b2][:], in0=ps_s[b][:],
                                 scalar=ob_sb[:, ot:ot + 1], in1=g_sb[b][:],
                                 op0=ALU.add, op1=ALU.mult)
        ve_tok[ot] = p.v.tensor_add(xo_sb[b2][:], tmp_sb[b2][:], x_sb[b][:]).inc(ve, 1)
        p.sy.wait(ve, ve_tok[ot])
        st_tok[b2] = p.sy.dma_start(out=rowsap(xoT, 128 * ot), in_=xo_sb[b2][:]).inc(sto, 16)
        if ot + 3 < 32:
            load_gx(ot + 3)
    p.t.matmul(ps_junk[0:1, :], ring["bufs"][31 % 3][:, 0:1], ctx_sb[:, 0:512],
               start=True, stop=True).inc(mm, 1)
    p.sy.wait(sto, sto.n)
    for cl in reversed(ps_cl):
        cl()
    p.pop_scope()
    return sto, sto.n



def _rope_tables(theta, scale):
    j = np.arange(64, dtype=np.float64)
    inv = theta ** (-2.0 * j / 128.0)
    t = np.arange(S, dtype=np.float64)
    f = np.outer(inv, t)
    return (np.cos(f) * scale).astype(np.float32), (np.sin(f) * scale).astype(np.float32)


def host_prep(inputs):
    d = {}
    d["ident"] = np.eye(128, dtype=np.float32).astype(BF)
    kk, qq = np.meshgrid(np.arange(128), np.arange(128), indexing="ij")
    d["tri"] = (kk <= qq).astype(np.float32).astype(BF)
    d["zerob"] = np.zeros((128, 128), dtype=np.float32).astype(BF)
    d["onescol"] = np.ones((128, 1), dtype=np.float32).astype(BF)
    d["onescolf"] = np.ones((128, 1), dtype=np.float32)
    d["onesrow"] = np.ones((1, 128), dtype=np.float32)
    d["ones8"] = np.ones((8, 1), dtype=np.float32)
    rep = np.zeros((8, 128), dtype=np.float32)
    for h in range(8):
        rep[h, 16 * h:16 * h + 16] = 1.0
    d["rep"] = rep
    d["shiftv"] = (4 * (np.arange(128) % 8)).astype(np.int32).reshape(128, 1)
    e32 = np.zeros((128, 32 * 32), dtype=np.float32)
    for t in range(32):
        e32[:, 32 * t + t] = 1.0
    d["e32"] = e32.astype(BF)
    d["csh0"] = np.full((128, 1), -C_SHIFT[0], dtype=np.float32)
    d["csh1"] = np.full((128, 1), -C_SHIFT[1], dtype=np.float32)

    cq0, sq0 = _rope_tables(10000.0, 1.0 / math.sqrt(HD))
    ck0, sk0 = _rope_tables(10000.0, 1.0)
    d["rope0"] = (cq0, sq0, ck0, sk0)
    cq1, sq1 = _rope_tables(1000000.0, 1.0 / HD)
    ck1, sk1 = _rope_tables(1000000.0, 1.0)
    d["rope1"] = (cq1, sq1, ck1, sk1)

    perm = np.concatenate([np.arange(0, 128, 2), np.arange(1, 128, 2)])
    wq = np.asarray(inputs["L0_attn_wq"])
    wk = np.asarray(inputs["L0_attn_wk"])
    wv = np.asarray(inputs["L0_attn_wv"])
    d["w0"] = []
    for c in range(NCORE):
        cols = []
        for w, permute in ((wq, True), (wk, True), (wv, False)):
            for j in range(4):
                h = 4 * c + j
                idx = h * 128 + (perm if permute else np.arange(128))
                cols.append(w[:, idx])
        d["w0"].append(np.ascontiguousarray(np.concatenate(cols, axis=1), dtype=np.float32))

    wqkv = np.asarray(inputs["L1_attn_w_qkv"])
    bqkv = np.asarray(inputs["L1_attn_b_qkv"])
    d["w1"] = []
    d["b1"] = []
    for c in range(NCORE):
        rows = np.arange(768 * c, 768 * (c + 1))
        d["w1"].append(np.ascontiguousarray(wqkv[rows].T, dtype=np.float32))
        d["b1"].append(np.ascontiguousarray(bqkv[rows].reshape(6, 128).T, dtype=np.float32))

    d["wrt"] = np.ascontiguousarray(np.asarray(inputs["L0_ffn_Wr"]).T, dtype=np.float32)
    d["at"] = np.ascontiguousarray(np.asarray(inputs["L0_ffn_A"]).reshape(128, D).T,
                                   dtype=np.float32)
    d["bt"] = np.ascontiguousarray(
        np.asarray(inputs["L0_ffn_B"]).transpose(0, 2, 1).reshape(128, D), dtype=np.float32)
    d["br"] = np.asarray(inputs["L0_ffn_br"], dtype=np.float32).reshape(8, 1)

    qw = np.asarray(inputs["L1_ffn_qweight"])
    d["qwrep"] = np.ascontiguousarray(np.repeat(qw, 8, axis=0))
    d["qzeros"] = np.asarray(inputs["L1_ffn_qzeros"])
    d["scales"] = np.asarray(inputs["L1_ffn_scales"], dtype=np.float32)
    d["fbias"] = np.ascontiguousarray(
        np.asarray(inputs["L1_ffn_bias"], dtype=np.float32).reshape(32, 128).T)

    wo0 = np.asarray(inputs["L0_attn_wo"], dtype=np.float32)
    wo1 = np.ascontiguousarray(np.asarray(inputs["L1_attn_w_dense"]).T, dtype=np.float32)
    d["wo0t"] = np.ascontiguousarray(
        wo0.reshape(32, 128, 32, 128).transpose(2, 1, 0, 3).reshape(D, D))
    d["wo1t"] = np.ascontiguousarray(
        wo1.reshape(32, 128, 32, 128).transpose(2, 1, 0, 3).reshape(D, D))
    d["ob0"] = np.zeros((128, 32), dtype=np.float32)
    d["ob1"] = np.ascontiguousarray(
        np.asarray(inputs["L1_attn_b_dense"], dtype=np.float32).reshape(32, 128).T)

    xT = np.asarray(inputs["L0_attn_x"], dtype=np.float32).reshape(S, D).T
    d["x"] = [np.ascontiguousarray(xT[:, T * c:T * (c + 1)]) for c in range(NCORE)]
    return d


CONST_KEYS = ("ident", "tri", "zerob", "onescol", "onescolf", "onesrow", "ones8", "rep",
              "shiftv", "e32", "csh0", "csh1")



def build_fused():
    p = P("mega")
    x_in = p.din("x", [D, T], F32)
    w0 = p.din("w0", [D, 1536], F32)
    w1 = p.din("w1", [D, 768], F32)
    bqkv1 = p.din("bqkv1", [128, 6], F32)
    wo0t = p.din("wo0t", [D, D], F32)
    wo1t = p.din("wo1t", [D, D], F32)
    ob0 = p.din("ob0", [128, 32], F32)
    ob1 = p.din("ob1", [128, 32], F32)
    wrt = p.din("wrt", [D, 8], F32)
    at = p.din("at", [D, 128], F32)
    bt = p.din("bt", [128, D], F32)
    br = p.din("br", [8, 1], F32)
    qwrep = p.din("qwrep", [D, D], I32)
    qzeros = p.din("qzeros", [32, 512], I32)
    scales = p.din("scales", [32, D], F32)
    fbias = p.din("fbias", [128, 32], F32)
    ropes = {}
    for l in (0, 1):
        for nm in ("cq", "sq", "ck", "sk"):
            ropes[(l, nm)] = p.din(f"{nm}{l}", [64, S], F32)
    xo_out = p.dout("xo", [D, T], F32)

    p.prefix = "c_"
    cled = p.sem("cled")
    io_c = {}
    for nm, shape, dt in (("ident", [128, 128], BF16), ("tri", [128, 128], BF16),
                          ("zerob", [128, 128], BF16), ("onescol", [128, 1], BF16),
                          ("onescolf", [128, 1], F32), ("onesrowf", [1, 128], F32),
                          ("csh0", [128, 1], F32), ("csh1", [128, 1], F32)):
        dnm = "onesrow" if nm == "onesrowf" else nm
        din = p.din(dnm, shape, dt)
        sbuf = p.sb(nm, shape, dt)
        p.sy.dma_start(out=sbuf[:], in_=din).inc(cled, 16)
        io_c[nm + "_sb"] = sbuf
        io_c[nm + "_din"] = din
    cld = (cled, cled.n)
    io_c["constld"] = cld

    ones8 = p.din("ones8", [8, 1], F32)
    rep = p.din("rep", [8, 128], F32)
    shiftv = p.din("shiftv", [128, 1], I32)
    e32 = p.din("e32", [128, 32 * 32], BF16)

    def mk(name, shape, dt):
        return p.dout(name, shape, dt) if name in TAPS else p.dram(name, shape, dt)

    xnT = [mk(f"i_xnT{l}", [D, T], BF16) for l in (0, 1)]
    xnTf = [p.dram(f"i_xnTf{l}", [NCORE * D, T], BF16) for l in (0, 1)]
    gT = [mk(f"i_gT{l}", [D, T], F32) for l in (0, 1)]
    qTd = [mk(f"i_qT{l}", [512, S], BF16) for l in (0, 1)]
    kTd = [mk("i_kT0", [512, S], BF16), mk("i_kT1", [128, S], BF16)]
    vTd = [mk("i_vT0", [512, S], BF16), mk("i_vT1", [128, S], BF16)]
    ctxa = [mk(f"i_ctxa{l}", [NCORE * 512, T], BF16) for l in (0, 1)]
    ctxsl = [p.dram(f"i_ctxsl{l}", [D, T], BF16) for l in (0, 1)]
    x1d = mk("i_x1", [D, T], F32)

    cc = [p.sem(f"cc{i}") for i in range(4)]

    prev_gate = ()
    for l in (0, 1):
        xin_l = x_in if l == 0 else x1d
        p.push_scope()
        xnT_sb = p.sb(f"xnTsb{l}", [128, 32 * T], BF16)
        lnio = {"onescolf": io_c["onescolf_sb"], "onesrow": io_c["onesrowf_sb"],
                "constld": cld}
        if f"dbg_xn{l}" in TAPS:
            lnio["dbg_xn"] = p.dout(f"dbg_xn{l}", [D, T], BF16)
        (stx, stx_n), xn_gate = emit_ln(p, l, xin_l, xnT_sb, xnT[l], lnio,
                                        gates=prev_gate)
        if l == 0:
            f_io = {"wrt": wrt, "at": at, "bt": bt, "br": br, "rep": rep,
                    "ones8": ones8, "onesrow": io_c["onesrowf_sb"]}
            if "dbg_ffn0" in TAPS:
                f_io["dbg"] = {"elog": p.dout("dbg_elog", [8, T], F32),
                               "scn": p.dout("dbg_scn", [128, T], F32),
                               "slow": p.dout("dbg_slow", [128, T], BF16)}
            stg, stg_n = emit_ffn0(p, xnT_sb, xn_gate, gT[l], f_io, ldgates=prev_gate)
        else:
            f_io = {"qwrep": qwrep, "qzeros": qzeros, "scales": scales,
                    "fbias": fbias, "shiftv": shiftv, "e32": e32}
            stg, stg_n = emit_ffn1(p, xnT_sb, xn_gate, gT[l], f_io, ldgates=prev_gate)
        p.prefix = f"c{l}_"
        p.g.wait(stx, stx_n)
        p.g.wait(stg, stg_n)
        p.g.collective_compute(
            "AllGather", ALU.bypass, replica_groups=[list(range(NCORE))],
            ins=[xnT[l]],
            outs=[xnTf[l]],
        ).inc(cc[2 * l], 1)
        p.pop_scope()

        qio = {"w": w0 if l == 0 else w1}
        for nm in ("cq", "sq", "ck", "sk"):
            qio[nm] = ropes[(l, nm)]
        if l == 1:
            qio["bqkv"] = bqkv1
        outs = {"qT": qTd[l], "kT": kTd[l], "vT": vTd[l]}
        qst, qst_n = emit_qkv(p, l, xnTf[l], outs, qio,
                              wgates=((stx, stx_n), (stg, stg_n)),
                              ccgate=(cc[2 * l], 1))

        ring = emit_oproj_ring(p, l, wo0t if l == 0 else wo1t, (qst, qst_n))

        aio = dict(io_c)
        aio.update({"qT": qTd[l], "kT": kTd[l], "vT": vTd[l], "ctxa": ctxa[l]})
        ast, ast_n = emit_attn(p, l, aio, gates=((qst, qst_n),))

        emit_ring_loads(p, l, ring, 0, 3)
        p.prefix = f"d{l}_"
        p.g.wait(ast, ast_n)
        p.g.collective_compute(
            "AllToAll", ALU.bypass, replica_groups=[list(range(NCORE))],
            ins=[ctxa[l]], outs=[ctxsl[l]],
        ).inc(cc[2 * l + 1], 1)
        emit_ring_loads(p, l, ring, 3, 32)

        oio = {"ctx_sl": ctxsl[l], "obias": ob0 if l == 0 else ob1, "gT": gT[l],
               "x": xin_l, "xo": x1d if l == 0 else xo_out,
               "g_gate": (stg, stg_n)}
        ost, ost_n = emit_oproj(p, l, oio, ring, ccgate=(cc[2 * l + 1], 1))
        for cl in reversed(ring["closers"]):
            cl()
        prev_gate = ((ost, ost_n),)
    return p.emit()


def kernel(**inputs):
    d = host_prep(inputs)
    cores = list(range(NCORE))
    with allow_ici_all_to_all():
        nc = build_fused()
        maps = []
        for c in cores:
            m = {k: d[k] for k in CONST_KEYS}
            m.update({"x": d["x"][c], "w0": d["w0"][c], "w1": d["w1"][c],
                      "bqkv1": d["b1"][c], "wo0t": d["wo0t"], "wo1t": d["wo1t"],
                      "ob0": d["ob0"], "ob1": d["ob1"], "wrt": d["wrt"], "at": d["at"],
                      "bt": d["bt"], "br": d["br"], "qwrep": d["qwrep"],
                      "qzeros": d["qzeros"], "scales": d["scales"], "fbias": d["fbias"]})
            for l in (0, 1):
                cq, sq, ck, sk = d[f"rope{l}"]
                m.update({f"cq{l}": cq, f"sq{l}": sq, f"ck{l}": ck, f"sk{l}": sk})
            maps.append(m)
        res = run_bass_kernel_spmd(nc, maps, core_ids=cores)
    y = np.concatenate([np.asarray(res.results[c]["xo"]) for c in cores], axis=1)
    return np.ascontiguousarray(y.T.reshape(1, S, D), dtype=np.float32)


# version 14
# speedup vs baseline: 1.0626x; 1.0626x over previous
import math
from contextlib import ExitStack

import numpy as np
import ml_dtypes

import concourse.bass as bass
import concourse.mybir as mybir
from concourse.ap import AP
from concourse.bass_utils import run_bass_kernel_spmd
from concourse.aot_env import allow_ici_all_to_all

F32 = mybir.dt.float32
BF16 = mybir.dt.bfloat16
I32 = mybir.dt.int32
AF = mybir.ActivationFunctionType
ALU = mybir.AluOpType

NCORE = 8
S = 8192
D = 4096
T = S // NCORE
HD = 128
C_SHIFT = {0: 20.0, 1: 5.0}
EPS = 1e-5

BF = ml_dtypes.bfloat16
TAPS = ()



class Sem:
    __slots__ = ("h", "n")

    def __init__(self, h):
        self.h = h
        self.n = 0


class OpTok:
    __slots__ = ("incs",)

    def __init__(self):
        self.incs = []

    def inc(self, sem, k=1):
        sem.n += k
        self.incs.append((sem, k))
        return sem.n


class EngRec:
    def __init__(self, prog, eng):
        self._p = prog
        self._e = eng

    def wait(self, sem, val):
        if val > 0:
            self._p.ops[self._e].append(("wait_ge", (sem.h, val), {}, OpTok()))

    def __getattr__(self, opname):
        p, e = self._p, self._e

        def call(*a, **k):
            tok = OpTok()
            p.ops[e].append((opname, a, k, tok))
            return tok
        return call


class P:

    def __init__(self, name):
        self.name = name
        self.nc = bass.Bass("TRN2", target_bir_lowering=False)
        self.ctx = ExitStack()
        self.ops = {e: [] for e in ("sync", "tensor", "vector", "scalar", "gpsimd")}
        self.t = EngRec(self, "tensor")
        self.v = EngRec(self, "vector")
        self.s = EngRec(self, "scalar")
        self.g = EngRec(self, "gpsimd")
        self.sy = EngRec(self, "sync")
        self.prefix = ""
        self._scopes = []

    def sem(self, name):
        return Sem(self.ctx.enter_context(self.nc.semaphore(self.prefix + name)))

    def _alloc(self, cm):
        h = cm.__enter__()
        if self._scopes:
            self._scopes[-1].append(cm)
        else:
            self.ctx.callback(lambda c=cm: None)
        return h

    def sb(self, name, shape, dt):
        return self._alloc(self.nc.sbuf_tensor(self.prefix + name, shape, dt))

    def psum(self, name, shape, dt):
        return self._alloc(self.nc.psum_tensor(self.prefix + name, shape, dt))

    def scoped_sb(self, name, shape, dt):
        cm = self.nc.sbuf_tensor(self.prefix + name, shape, dt)
        h = cm.__enter__()
        return h, (lambda: cm.__exit__(None, None, None))

    def scoped_psum(self, name, shape, dt):
        cm = self.nc.psum_tensor(self.prefix + name, shape, dt)
        h = cm.__enter__()
        return h, (lambda: cm.__exit__(None, None, None))

    def push_scope(self):
        self._scopes.append([])

    def pop_scope(self):
        for cm in reversed(self._scopes.pop()):
            cm.__exit__(None, None, None)

    def din(self, name, shape, dt):
        return self.nc.dram_tensor(name, shape, dt, kind="ExternalInput").ap()

    def dout(self, name, shape, dt):
        return self.nc.dram_tensor(name, shape, dt, kind="ExternalOutput").ap()

    def dram(self, name, shape, dt, addr_space="Local"):
        return self.nc.dram_tensor(name, shape, dt, kind="Internal",
                                   addr_space=addr_space).ap()

    def emit(self):
        blk = self.ctx.enter_context(self.nc.Block())

        def runner(eng_name):
            def run(eng):
                for (opname, a, k, tok) in self.ops[eng_name]:
                    inst = getattr(eng, opname)(*a, **k)
                    for (sem, kk) in tok.incs:
                        inst.then_inc(sem.h, kk)
            return run

        for eng_name, method in (("sync", blk.sync), ("tensor", blk.tensor),
                                 ("vector", blk.vector), ("scalar", blk.scalar),
                                 ("gpsimd", blk.gpsimd)):
            if self.ops[eng_name]:
                method(runner(eng_name))
        self.ctx.close()
        return self.nc


def pview(dram_ap, r0, nrows, c0, ncols, p=128):
    t = dram_ap.tensor
    C = t.shape[-1]
    off = dram_ap.offset + r0 * C + c0
    return AP(t, off, [[C, p], [p * C, nrows // p], [1, ncols]])


def gload(p, dst_sb, dram_ap, r0, nrows, c0, ncols, sem, dst_c0=0, tiles_per_dma=4,
          eng=None):
    eng = eng or p.g
    nt = nrows // 128
    for t0 in range(0, nt, tiles_per_dma):
        tcnt = min(tiles_per_dma, nt - t0)
        src = pview(dram_ap, r0 + 128 * t0, 128 * tcnt, c0, ncols)
        dst = dst_sb[:, dst_c0 + t0 * ncols: dst_c0 + (t0 + tcnt) * ncols]
        eng.dma_start(out=dst, in_=src).inc(sem, 16)
    return sem.n


def bcast_row(dram_ap, row, col0, ncols, nparts=128):
    t = dram_ap.tensor
    rowlen = t.shape[-1]
    off = dram_ap.offset + row * rowlen + col0
    return AP(t, off, [[0, nparts], [1, ncols]])


def rowsap(dram_ap, r0, nrows=128):
    t = dram_ap.tensor
    C = t.shape[-1]
    return AP(t, dram_ap.offset + r0 * C, [[C, nrows], [1, C]])



def emit_ln(p, layer, xT_in, xnT_sb, xnT_out, io, gates=()):
    p.prefix = f"ln{layer}_"
    ld = p.sem("ld")
    mm = p.sem("mm")
    ve = p.sem("ve")
    sc = p.sem("sc")
    stx = p.sem("stx")

    x_sb = [p.sb(f"x{i}", [128, T], F32) for i in range(3)]
    xq_sb = [p.sb(f"xq{i}", [128, T], F32) for i in range(2)]
    m_sb = p.sb("m", [1, T], F32)
    e2_sb = p.sb("e2", [1, T], F32)
    var_sb = p.sb("var", [1, T], F32)
    a_sb = p.sb("a", [1, T], F32)
    b_sb = p.sb("b", [1, T], F32)
    ab_sb = p.sb("ab", [128, T], F32)
    bb_sb = p.sb("bb", [128, T], F32)
    tmp_sb = p.sb("tmp", [128, T], F32)

    ps_sx, cl1 = p.scoped_psum("ps_sx", [1, T], F32)
    ps_sq, cl0 = p.scoped_psum("ps_sq", [1, T], F32)
    ps_a, cl2 = p.scoped_psum("ps_a", [128, T], F32)
    ps_b, cl3 = p.scoped_psum("ps_b", [128, T], F32)

    onescolf, onesrow = io["onescolf"], io["onesrow"]
    cld = io["constld"]

    mm_tok = [0] * 32
    ld_tok = [0] * 32
    for ct in range(32):
        b = ct % 3
        for (sm, val) in gates:
            if ct < 3:
                p.sy.wait(sm, val)
        if ct >= 3:
            p.sy.wait(mm, mm_tok[ct - 3])
        ld_tok[ct] = p.sy.dma_start(out=x_sb[b][:],
                                    in_=rowsap(xT_in, 128 * ct)).inc(ld, 16)
        b2 = ct % 2
        p.v.wait(ld, ld_tok[ct])
        if ct >= 2:
            p.v.wait(mm, mm_tok[ct - 2])
        ve_sq = p.v.tensor_mul(xq_sb[b2][:], x_sb[b][:], x_sb[b][:]).inc(ve, 1)
        p.t.wait(ld, ld_tok[ct])
        p.t.wait(ve, ve_sq)
        if ct == 0:
            p.t.wait(cld[0], cld[1])
        tok = None
        for th in range(2):
            p.t.matmul(ps_sx[0:1, th * 512:(th + 1) * 512], onescolf[:],
                       x_sb[b][:, th * 512:(th + 1) * 512],
                       start=(ct == 0), stop=(ct == 31))
            tok = p.t.matmul(ps_sq[0:1, th * 512:(th + 1) * 512], onescolf[:],
                             xq_sb[b2][:, th * 512:(th + 1) * 512],
                             start=(ct == 0), stop=(ct == 31))
        mm_tok[ct] = tok.inc(mm, 1)
    dmy = p.t.matmul(ps_a[0:1, 0:512], onescolf[:], x_sb[31 % 3][:, 0:512],
                     start=True, stop=True).inc(mm, 1)
    p.s.wait(mm, dmy)
    p.s.copy(m_sb[:], ps_sx[:])
    sc_ev = p.s.copy(e2_sb[:], ps_sq[:]).inc(sc, 1)

    p.v.wait(sc, sc_ev)
    p.v.tensor_scalar(out=m_sb[:], in0=m_sb[:], scalar1=1.0 / D, scalar2=None,
                      op0=ALU.mult)
    p.v.tensor_scalar(out=e2_sb[:], in0=e2_sb[:], scalar1=1.0 / D, scalar2=None,
                      op0=ALU.mult)
    p.v.tensor_mul(var_sb[:], m_sb[:], m_sb[:])
    p.v.tensor_sub(var_sb[:], e2_sb[:], var_sb[:])
    ve_v = p.v.tensor_scalar(out=var_sb[:], in0=var_sb[:], scalar1=EPS, scalar2=None,
                             op0=ALU.add).inc(ve, 1)
    p.s.wait(ve, ve_v)
    p.s.activation(out=var_sb[:], in_=var_sb[:], func=AF.Ln)
    sc_a = p.s.activation(out=a_sb[:], in_=var_sb[:], func=AF.Exp, scale=-0.5).inc(sc, 1)
    p.v.wait(sc, sc_a)
    ve_b = p.v.scalar_tensor_tensor(out=b_sb[:], in0=m_sb[:], scalar=-1.0, in1=a_sb[:],
                                    op0=ALU.mult, op1=ALU.mult).inc(ve, 1)
    p.t.wait(ve, ve_b)
    for th in range(2):
        p.t.matmul(ps_a[:, th * 512:(th + 1) * 512], onesrow[:],
                   a_sb[0:1, th * 512:(th + 1) * 512], start=True, stop=True)
        tok = p.t.matmul(ps_b[:, th * 512:(th + 1) * 512], onesrow[:],
                         b_sb[0:1, th * 512:(th + 1) * 512], start=True, stop=True)
    tok.inc(mm, 1)
    dmy2 = p.t.matmul(ps_sx[0:1, 0:512], onescolf[:], x_sb[31 % 3][:, 0:512],
                      start=True, stop=True).inc(mm, 1)
    p.v.wait(mm, dmy2)
    p.v.tensor_copy(ab_sb[:], ps_a[:])
    ve_bc = p.v.tensor_copy(bb_sb[:], ps_b[:]).inc(ve, 1)

    xn_done = [0] * 32
    ld2_tok = [0] * 32

    def load2(ct):
        b = ct % 3
        if ct < 3:
            p.sy.wait(mm, mm_tok[31])
        else:
            p.sy.wait(ve, xn_done[ct - 3])
        ld2_tok[ct] = p.sy.dma_start(out=x_sb[b][:],
                                     in_=rowsap(xT_in, 128 * ct)).inc(ld, 16)

    load2(0)
    for ct in range(32):
        b = ct % 3
        if ct + 1 < 32:
            load2(ct + 1)
        p.v.wait(ld, ld2_tok[ct])
        if ct == 0:
            p.v.wait(ve, ve_bc)
        p.v.tensor_mul(tmp_sb[:], x_sb[b][:], ab_sb[:])
        xn_done[ct] = p.v.tensor_add(xnT_sb[:, ct * T:(ct + 1) * T], tmp_sb[:],
                                     bb_sb[:]).inc(ve, 1)
        if ct % 4 == 3:
            g4 = ct // 4
            p.sy.wait(ve, xn_done[ct] + 1)
            p.sy.dma_start(
                out=pview(xnT_out, 128 * (g4 * 4), 512, 0, T),
                in_=xnT_sb[:, (g4 * 4) * T:(g4 * 4 + 4) * T]).inc(stx, 16)
    p.v.tensor_copy(tmp_sb[0:1, 0:8], ab_sb[0:1, 0:8]).inc(ve, 1)
    if io.get("dbg_xn") is not None:
        for g4 in range(8):
            p.sy.dma_start(out=pview(io["dbg_xn"], 128 * (g4 * 4), 512, 0, T),
                           in_=xnT_sb[:, (g4 * 4) * T:(g4 * 4 + 4) * T]).inc(stx, 16)
    cl3()
    cl2()
    cl0()
    cl1()
    return (stx, stx.n), (ve, xn_done[31])



def emit_ffn0(p, xnT_sb, xn_gate, gT_out, io, ldgates=()):
    p.prefix = "f0_"
    wrt, at, bt = io["wrt"], io["at"], io["bt"]
    br, rep, ones8, onesrow = io["br"], io["rep"], io["ones8"], io["onesrow"]

    ld = p.sem("ld")
    ldg = p.sem("ldg")
    mm = p.sem("mm")
    ve = p.sem("ve")
    sc = p.sem("sc")
    st = p.sem("st")

    wrt_bf = p.sb("wrt", [128, 32 * 8], BF16)
    at_bf = p.sb("at", [128, 32 * 128], BF16)
    bt_bf = p.sb("bt", [128, D], BF16)
    br_sb = p.sb("br", [8, 1], F32)
    rep_sb = p.sb("rep", [8, 128], F32)
    ones8_sb = p.sb("ones8", [8, 1], F32)
    elog = p.sb("elog", [8, T], F32)
    recip = p.sb("recip", [1, T], F32)
    rb_sb = p.sb("rb", [128, T], F32)
    scn = p.sb("scn", [128, T], F32)
    slow = p.sb("slow", [128, T], BF16)
    g_sb = [p.sb(f"g{i}", [128, T], BF16) for i in range(2)]

    ps_log, cl1 = p.scoped_psum("pslog", [8, T], F32)
    ps_a, cl2 = p.scoped_psum("psa", [128, T], F32)
    ps_b, cl3 = p.scoped_psum("psb", [128, T], F32)
    ps_low, cl4 = p.scoped_psum("pslow", [128, T], F32)

    for (sm, val) in ldgates:
        p.sy.wait(sm, val)
        p.g.wait(sm, val)
    gload(p, wrt_bf, wrt, 0, D, 0, 8, ldg)
    gload(p, at_bf, at, 0, D, 0, 128, ldg)
    t3 = p.g.dma_start(out=bt_bf[:], in_=bt).inc(ldg, 16)
    p.sy.dma_start(out=br_sb[:], in_=br).inc(ld, 16)
    p.sy.dma_start(out=rep_sb[:], in_=rep).inc(ld, 16)
    t7 = p.sy.dma_start(out=ones8_sb[:], in_=ones8).inc(ld, 16)

    p.t.wait(ldg, t3)
    p.t.wait(ld, t7)
    p.t.wait(xn_gate[0], xn_gate[1])
    tok = None
    for th in range(2):
        for ct in range(32):
            tok = p.t.matmul(ps_log[:, th * 512:(th + 1) * 512],
                             wrt_bf[:, ct * 8:(ct + 1) * 8],
                             xnT_sb[:, ct * T + th * 512: ct * T + (th + 1) * 512],
                             start=(ct == 0), stop=(ct == 31))
    tok.inc(mm, 1)
    dmy1 = p.t.matmul(ps_a[0:1, 0:512], wrt_bf[:, 0:1], xnT_sb[:, 0:512],
                      start=True, stop=True).inc(mm, 1)
    p.s.wait(mm, dmy1)
    sc_exp = p.s.activation(out=elog[:], in_=ps_log[:], func=AF.Exp, bias=br_sb[:, 0:1]).inc(sc, 1)
    p.t.wait(sc, sc_exp)
    for th in range(2):
        tok = p.t.matmul(ps_log[0:1, th * 512:(th + 1) * 512], ones8_sb[:],
                         elog[:, th * 512:(th + 1) * 512], start=True, stop=True)
    tok.inc(mm, 1)
    for th in range(2):
        tok = p.t.matmul(ps_a[:, th * 512:(th + 1) * 512], rep_sb[:],
                         elog[:, th * 512:(th + 1) * 512], start=True, stop=True)
    mm_sc = tok.inc(mm, 1)
    p.v.wait(mm, mm_sc)
    ve_t = p.v.reciprocal(recip[:], ps_log[0:1, :]).inc(ve, 1)
    p.v.wait(ve, ve_t)
    ve_r16 = p.v.tensor_scalar(out=recip[:], in0=recip[:], scalar1=16.0, scalar2=None,
                               op0=ALU.mult).inc(ve, 1)
    p.t.wait(ve, ve_r16)
    for th in range(2):
        tok = p.t.matmul(ps_b[:, th * 512:(th + 1) * 512], onesrow[:],
                         recip[:, th * 512:(th + 1) * 512], start=True, stop=True)
    mm_rb = tok.inc(mm, 1)
    for th in range(2):
        for ct in range(32):
            tok = p.t.matmul(ps_low[:, th * 512:(th + 1) * 512],
                             at_bf[:, ct * 128:(ct + 1) * 128],
                             xnT_sb[:, ct * T + th * 512: ct * T + (th + 1) * 512],
                             start=(ct == 0), stop=(ct == 31))
    mm_low = tok.inc(mm, 1)
    dmy2 = p.t.matmul(ps_log[0:1, 0:512], wrt_bf[:, 0:1], xnT_sb[:, 0:512],
                      start=True, stop=True).inc(mm, 1)
    p.s.wait(mm, mm_low)
    sc_rb = p.s.copy(rb_sb[:], ps_b[:]).inc(sc, 1)
    p.v.wait(sc, sc_rb)
    ve_scn = p.v.tensor_mul(scn[:], ps_a[:], rb_sb[:]).inc(ve, 1)
    p.v.wait(mm, dmy2)
    p.v.wait(ve, ve_scn)
    ve_slow = p.v.tensor_mul(slow[:], ps_low[:], scn[:]).inc(ve, 1)
    if io.get("dbg") is not None:
        p.sy.wait(sc, sc_exp)
        p.sy.dma_start(out=io["dbg"]["elog"], in_=elog[:]).inc(st, 16)
        p.sy.wait(ve, ve_scn)
        p.sy.dma_start(out=io["dbg"]["scn"], in_=scn[:]).inc(st, 16)
        p.sy.wait(ve, ve_slow)
        p.sy.dma_start(out=io["dbg"]["slow"], in_=slow[:]).inc(st, 16)
    p.t.wait(ve, ve_slow)
    st_tok = [0, 0]
    sc_t = [0] * 33
    for ot in range(32):
        b = ot % 2
        pg = ps_a if b == 0 else ps_b
        if ot >= 2:
            p.t.wait(sc, sc_t[ot - 1])
        for th in range(2):
            tok = p.t.matmul(pg[:, th * 512:(th + 1) * 512],
                             bt_bf[:, ot * 128:(ot + 1) * 128],
                             slow[:, th * 512:(th + 1) * 512], start=True, stop=True)
        mm_t = tok.inc(mm, 1)
        p.s.wait(mm, mm_t)
        p.s.wait(st, st_tok[b])
        sc_t[ot] = p.s.activation(out=g_sb[b][:], in_=pg[:], func=AF.Silu).inc(sc, 1)
        if ot >= 1:
            p.sy.wait(sc, sc_t[ot])
            st_tok[(ot - 1) % 2] = p.sy.dma_start(
                out=gT_out[(ot - 1) * 128:ot * 128, :],
                in_=g_sb[(ot - 1) % 2][:]).inc(st, 16)
    sc_t[32] = p.s.copy(rb_sb[0:1, 0:8], scn[0:1, 0:8]).inc(sc, 1)
    p.sy.wait(sc, sc_t[32])
    p.sy.dma_start(out=gT_out[31 * 128:32 * 128, :], in_=g_sb[31 % 2][:]).inc(st, 16)
    cl4()
    cl3()
    cl2()
    cl1()
    return st, st.n


def emit_ffn1(p, xnT_sb, xn_gate, gT_out, io, ldgates=()):
    p.prefix = "f1_"
    qwrep, qzeros, scales = io["qwrep"], io["qzeros"], io["scales"]
    fbias, shiftv, e32 = io["fbias"], io["shiftv"], io["e32"]

    ld = p.sem("ld")
    ldg = p.sem("ldg")
    mm = p.sem("mm")
    ve = p.sem("ve")
    sc = p.sem("sc")
    st = p.sem("st")
    dq = p.sem("dq")

    qz_sb = p.sb("qz", [32, 512], I32)
    zr_sb = p.sb("zr", [32, D], I32)
    sc_sb = p.sb("scales", [32, D], F32)
    negsz = p.sb("negsz", [32, D], F32)
    fbias_sb = p.sb("fb", [128, 32], F32)
    shv_sb = p.sb("shv", [128, 1], I32)
    e32_sb = p.sb("e32", [128, 32 * 32], BF16)
    xnsum = p.sb("xnsum", [32, T], F32)
    qw_sb = [p.sb(f"qw{i}", [128, 512], I32) for i in range(4)]
    sbc_sb = [p.sb(f"sbc{i}", [128, 512], F32) for i in range(4)]
    wnib = [p.sb(f"wn{i}", [128, 512], I32) for i in range(2)]
    w_bf = [p.sb(f"w{i}", [128, 512], BF16) for i in range(2)]
    g_sb = [p.sb(f"g{i}", [128, T], BF16) for i in range(2)]

    ps_cl = []
    ps_g = []
    for i in range(8):
        h, cl = p.scoped_psum(f"psg{i}", [128, 512], F32)
        ps_g.append(h)
        ps_cl.append(cl)

    for (sm, val) in ldgates:
        p.sy.wait(sm, val)
    t1 = p.sy.dma_start(out=qz_sb[:], in_=qzeros).inc(ld, 16)
    t2 = p.sy.dma_start(out=sc_sb[:], in_=scales).inc(ld, 16)
    t3 = p.sy.dma_start(out=fbias_sb[:], in_=fbias).inc(ld, 16)
    t4 = p.sy.dma_start(out=shv_sb[:], in_=shiftv).inc(ld, 16)
    t5 = p.sy.dma_start(out=e32_sb[:], in_=e32).inc(ld, 16)

    p.v.wait(ld, t2)
    ve_t = 0
    for i in range(8):
        zrview = zr_sb[:].rearrange("g (j i) -> g j i", i=8)[:, :, i]
        ve_t = p.v.tensor_scalar(out=zrview, in0=qz_sb[:], scalar1=4 * i, scalar2=15,
                                 op0=ALU.logical_shift_right, op1=ALU.bitwise_and).inc(ve, 1)
    p.v.wait(ve, ve_t)
    ve_negsz = p.v.scalar_tensor_tensor(out=negsz[:], in0=zr_sb[:], scalar=-1.0, in1=sc_sb[:],
                                        op0=ALU.mult, op1=ALU.mult).inc(ve, 1)
    p.t.wait(ld, t5)
    p.t.wait(xn_gate[0], xn_gate[1])
    tok = None
    for th in range(2):
        for t in range(32):
            tok = p.t.matmul(ps_g[th][0:32, :],
                             e32_sb[:, t * 32:(t + 1) * 32],
                             xnT_sb[:, t * T + th * 512: t * T + (th + 1) * 512],
                             start=(t == 0), stop=(t == 31))
    tok.inc(mm, 1)
    dmy = p.t.matmul(ps_g[2][0:1, :], e32_sb[:, 0:1], xnT_sb[:, 0:512],
                     start=True, stop=True).inc(mm, 1)
    p.s.wait(mm, dmy)
    p.s.copy(xnsum[:, 0:512], ps_g[0][0:32, :])
    p.s.copy(xnsum[:, 512:1024], ps_g[1][0:32, :])
    sc_xn = p.s.copy(g_sb[0][0:32, 0:512], xnsum[0:32, 0:512]).inc(sc, 1)

    st_of = {}
    dq_tok = {}
    ve_tok = {}
    og_st = 0
    pending = None
    idx = 0
    for og in range(8):
        of0 = og * 4
        for t in range(32):
            b4 = idx % 4
            b = idx % 2
            if idx >= 2:
                p.sy.wait(dq, dq_tok[idx - 2])
            p.sy.dma_start(out=qw_sb[b4][:],
                           in_=qwrep[128 * t:128 * (t + 1), 512 * og:512 * (og + 1)]).inc(ldg, 16)
            lt2 = p.sy.dma_start(out=sbc_sb[b4][:],
                                 in_=bcast_row(scales, t, 512 * og, 512)).inc(ldg, 16)
            p.v.wait(ldg, lt2)
            if idx == 0:
                p.v.wait(ld, t4)
            if idx >= 2:
                p.v.wait(dq, dq_tok[idx - 2])
            ve_t = p.v.tensor_scalar(out=wnib[b][:], in0=qw_sb[b4][:], scalar1=shv_sb[:],
                                     scalar2=15, op0=ALU.logical_shift_right,
                                     op1=ALU.bitwise_and).inc(ve, 1)
            p.v.wait(ve, ve_t)
            ve_tok[idx] = p.v.tensor_mul(w_bf[b][:], wnib[b][:], sbc_sb[b4][:]).inc(ve, 1)
            p.t.wait(ve, ve_tok[idx])
            if t == 0:
                if og > 0:
                    p.t.wait(st, og_st)
                else:
                    p.t.wait(sc, sc_xn)
            tok = None
            for q in range(4):
                for th in range(2):
                    tok = p.t.matmul(ps_g[q * 2 + th][:],
                                     w_bf[b][:, 128 * q:128 * (q + 1)],
                                     xnT_sb[:, t * T + th * 512: t * T + (th + 1) * 512],
                                     start=(t == 0), stop=False)
            dq_tok[idx] = tok.inc(dq, 1)
            idx += 1
        p.t.wait(ve, ve_negsz)
        for q in range(4):
            for th in range(2):
                tok = p.t.matmul(ps_g[q * 2 + th][:],
                                 negsz[:, 128 * (of0 + q):128 * (of0 + q + 1)],
                                 xnsum[:, th * 512:(th + 1) * 512],
                                 start=False, stop=True)
        mm_t = tok.inc(mm, 1)
        p.s.wait(mm, mm_t)
        if og == 0:
            p.s.wait(ld, t3)
        for q in range(4):
            of = of0 + q
            qb = (og * 4 + q) % 2
            p.s.wait(st, st_of.get(qb, 0))
            sc_t = None
            for th in range(2):
                sc_t = p.s.activation(out=g_sb[qb][:, th * 512:(th + 1) * 512],
                                      in_=ps_g[q * 2 + th][:], func=AF.Silu,
                                      bias=fbias_sb[:, of:of + 1]).inc(sc, 1)
            if pending is not None:
                pof, pqb = pending
                p.sy.wait(sc, sc_t)
                st_of[pqb] = p.sy.dma_start(
                    out=gT_out[pof * 128:(pof + 1) * 128, :],
                    in_=g_sb[pqb][:]).inc(st, 16)
            pending = (of, qb)
        og_st = st.n
    sc_d = p.s.copy(xnsum[0:1, 0:8], fbias_sb[0:1, 0:8]).inc(sc, 1)
    pof, pqb = pending
    p.sy.wait(sc, sc_d)
    p.sy.dma_start(out=gT_out[pof * 128:(pof + 1) * 128, :],
                   in_=g_sb[pqb][:]).inc(st, 16)
    for cl in reversed(ps_cl):
        cl()
    return st, st.n



def emit_qkv(p, layer, xnTf, outs, io, wgates=(), ccgate=None):
    p.prefix = f"q{layer}_"
    p.push_scope()
    wcols = {0: 1536, 1: 768}[layer]
    nob = {0: 12, 1: 6}[layer]
    w_in = io["w"]
    cq, sq, ck, sk = io["cq"], io["sq"], io["ck"], io["sk"]

    ld = p.sem("ld")
    ldw = p.sem("ldw")
    xl = p.sem("xl")
    xl2 = p.sem("xl2")
    mm = p.sem("mm")
    ve = p.sem("ve")
    sc = p.sem("sc")
    st = p.sem("st")
    po = p.sem("po")

    w_sb = p.sb("w", [128, 32 * wcols], BF16)
    xn_sb = [p.sb(f"xn{i}", [128, 32 * 512], BF16) for i in range(2)]
    rope_sb = [p.sb(f"rope{i}", [128, 512], F32) for i in range(4)]
    ev_sb = [p.sb(f"ev{i}", [128, 512], F32) for i in range(2)]
    esw_sb = [p.sb(f"esw{i}", [128, 512], F32) for i in range(2)]
    t1_sb = [p.sb(f"t1{i}", [128, 512], F32) for i in range(2)]
    t2_sb = [p.sb(f"t2{i}", [128, 512], F32) for i in range(2)]
    out_sb = [p.sb(f"out{i}", [128, 512], BF16) for i in range(4)]
    if layer == 1:
        bq_sb = p.sb("bq", [128, 6], F32)
    ps_cl = []
    ps = []
    for i in range(6):
        h, cl = p.scoped_psum(f"ps{i}", [128, 512], F32)
        ps.append(h)
        ps_cl.append(cl)
    ps_junk, cl_junk = p.scoped_psum("psjunk", [1, 512], F32)
    ps_cl.append(cl_junk)
    nglobal = 16 * nob

    for (sm, val) in wgates:
        p.g.wait(sm, val)
        p.sy.wait(sm, val)
    if ccgate is not None:
        p.sy.wait(ccgate[0], ccgate[1])
    wt = gload(p, w_sb, w_in, 0, D, 0, wcols, ldw)
    if layer == 1:
        tb = p.sy.dma_start(out=bq_sb[:], in_=io["bqkv"]).inc(ld, 16)
    p.t.wait(ldw, wt)
    if layer == 1:
        p.t.wait(ld, tb)

    if layer == 0:
        kinds = ["q"] * 4 + ["k"] * 4 + ["v"] * 4
        head_of_ob = [0, 1, 2, 3, 0, 1, 2, 3, 0, 1, 2, 3]
    else:
        kinds = ["q"] * 4 + ["k", "v"]
        head_of_ob = [0, 1, 2, 3, 0, 0]

    mm_end = [0] * 16
    sc_tok = {}
    st_tok = [0] * 4
    ve_tok = {}
    po_last = 0
    ve4_tok = [0, 0]
    rt_toks = [0] * 16
    gidx = 0
    xt_tok = [0] * 16
    xt2_tok = [0] * 16

    def load_xn(tc):
        xb = tc % 2
        r, th = tc // 2, tc % 2
        if tc < 2:
            if ccgate is not None:
                p.sy.wait(ccgate[0], ccgate[1])
                p.s.wait(ccgate[0], ccgate[1])
        else:
            p.sy.wait(mm, mm_end[tc - 2])
            p.s.wait(mm, mm_end[tc - 2])
        xt_tok[tc] = gload(p, xn_sb[xb], xnTf, r * D, D // 2, th * 512, 512, xl,
                           eng=p.sy)
        xt2_tok[tc] = gload(p, xn_sb[xb], xnTf, r * D + D // 2, D // 2, th * 512,
                            512, xl2, dst_c0=16 * 512, eng=p.s)

    load_xn(0)
    load_xn(1)
    for tc in range(16):
        xb = tc % 2
        if tc >= 1:
            p.sy.wait(ve, ve_tok.get(tc - 1, 0))
        rt_t = None
        for i, tabap in enumerate((cq, sq, ck, sk)):
            p.sy.dma_start(out=rope_sb[i][0:64, :],
                           in_=tabap[:, tc * 512:(tc + 1) * 512]).inc(ld, 16)
            rt_t = p.sy.dma_start(out=rope_sb[i][64:128, :],
                                  in_=tabap[:, tc * 512:(tc + 1) * 512]).inc(ld, 16)
        rt_toks[tc] = rt_t
        p.t.wait(xl, xt_tok[tc])
        p.t.wait(xl2, xt2_tok[tc])
        for ob in range(nob):
            pb = ps[ob % 6]
            if gidx >= 5:
                p.t.wait(sc, sc_tok[gidx - 5])
            tok = None
            for ct in range(32):
                tok = p.t.matmul(pb[:],
                                 w_sb[:, ct * wcols + ob * 128: ct * wcols + (ob + 1) * 128],
                                 xn_sb[xb][:, ct * 512:(ct + 1) * 512],
                                 start=(ct == 0), stop=(ct == 31))
            mm_t = tok.inc(mm, 1)
            chase = mm_t + 1 if gidx < nglobal - 1 else nglobal + 1
            kind = kinds[ob]
            h = head_of_ob[ob]
            obuf = ob % 4 if layer == 0 else (ob if ob < 4 else ob - 2)
            obuf = obuf % 4
            if kind == "v":
                ob_buf = out_sb[obuf]
                p.s.wait(mm, chase)
                p.s.wait(st, st_tok[obuf])
                if layer == 1:
                    sc_t = p.s.add(ob_buf[:], pb[:], bq_sb[:, 5:6]).inc(sc, 1)
                else:
                    sc_t = p.s.copy(ob_buf[:], pb[:]).inc(sc, 1)
                sc_tok[gidx] = sc_t
                p.sy.wait(sc, sc_t + 1)
                st_tok[obuf] = p.sy.dma_start(
                    out=outs["vT"][h * 128:(h + 1) * 128, tc * 512:(tc + 1) * 512],
                    in_=ob_buf[:]).inc(st, 16)
            else:
                e2 = ob % 2
                eb = ev_sb[e2]
                p.s.wait(mm, chase)
                p.s.wait(ve, ve4_tok[e2])
                if layer == 1:
                    bias = bq_sb[:, ob:ob + 1] if kind == "q" else bq_sb[:, 4:5]
                    sc_t = p.s.add(eb[:], pb[:], bias).inc(sc, 1)
                else:
                    sc_t = p.s.copy(eb[:], pb[:]).inc(sc, 1)
                sc_tok[gidx] = sc_t
                c_t = rope_sb[0] if kind == "q" else rope_sb[2]
                s_t = rope_sb[1] if kind == "q" else rope_sb[3]
                tb1 = t1_sb[e2]
                tb2 = t2_sb[e2]
                esw = esw_sb[e2]
                ob_buf = out_sb[obuf]
                p.g.wait(sc, sc_t)
                p.g.wait(ve, ve4_tok[e2])
                p.g.tensor_copy(esw[0:64, :], eb[64:128, :])
                gsw = p.g.tensor_copy(esw[64:128, :], eb[0:64, :]).inc(po, 1)
                p.v.wait(sc, sc_t)
                p.v.wait(po, gsw)
                p.v.wait(ld, rt_toks[tc])
                p.v.wait(st, st_tok[obuf])
                p.v.tensor_mul(tb1[:], eb[:], c_t[:])
                p.v.tensor_mul(tb2[:], esw[:], s_t[:]).inc(ve, 1)
                p.v.tensor_sub(ob_buf[0:64, :], tb1[0:64, :], tb2[0:64, :])
                p.v.tensor_add(ob_buf[64:128, :], tb1[64:128, :], tb2[64:128, :]).inc(ve, 1)
                ve_4 = p.v.tensor_copy(tb1[0:1, 0:8], tb2[0:1, 0:8]).inc(ve, 1)
                ve4_tok[e2] = ve_4
                ve_tok[tc] = ve_4
                p.sy.wait(ve, ve_4)
                dst = outs["qT"] if kind == "q" else outs["kT"]
                st_tok[obuf] = p.sy.dma_start(
                    out=dst[h * 128:(h + 1) * 128, tc * 512:(tc + 1) * 512],
                    in_=ob_buf[:]).inc(st, 16)
            gidx += 1
        mm_end[tc] = mm.n
        if tc + 2 < 16:
            load_xn(tc + 2)
    p.t.matmul(ps_junk[0:1, :], w_sb[:, 0:1], xn_sb[1][:, 0:512],
               start=True, stop=True).inc(mm, 1)
    p.v.tensor_copy(xn_sb[0][0:1, 0:8], rope_sb[0][0:1, 0:8]).inc(ve, 1)
    p.s.copy(xn_sb[0][0:1, 8:16], rope_sb[0][0:1, 0:8]).inc(sc, 1)
    for cl in reversed(ps_cl):
        cl()
    p.pop_scope()
    return st, st.n



def attn_pieces(layer, qc):
    out = []
    q_lo_t, q_hi_t = 4 * qc, 4 * qc + 3
    if layer == 0:
        for kt in range(0, q_hi_t + 1):
            col_lo = max(0, 128 * (kt - q_lo_t))
            diag = col_lo if kt >= q_lo_t else None
            out.append((kt, "full", col_lo, 512, diag))
    else:
        for kt in range(0, q_hi_t + 1):
            lo_t = max(q_lo_t, kt)
            hi_t = min(q_hi_t, kt + 7)
            if lo_t <= hi_t:
                col_lo = 128 * (lo_t - q_lo_t)
                col_hi = 128 * (hi_t - q_lo_t) + 128
                diag = 128 * (kt - q_lo_t) if kt >= q_lo_t else None
                out.append((kt, "full", col_lo, col_hi, diag))
            if kt % 4 == 3:
                lo_t = max(q_lo_t, kt + 8)
                if lo_t <= q_hi_t:
                    out.append((kt, "upper", 128 * (lo_t - q_lo_t), 512, None))
            else:
                i = kt + 8
                if q_lo_t <= i <= q_hi_t:
                    c0 = 128 * (i - q_lo_t)
                    out.append((kt, "upper", c0, c0 + 64, None))
    return out


def emit_attn(p, layer, io, gates=()):
    p.prefix = f"a{layer}_"
    p.push_scope()
    qT, kT, vT, ctxa = io["qT"], io["kT"], io["vT"], io["ctxa"]
    lrowd = p.dram(f"a{layer}_lrowd", [2, 512], F32)

    ld = p.sem("ld")
    ldh = p.sem("ldh")
    mm = p.sem("mm")
    ve = p.sem("ve")
    sc = p.sem("sc")
    st = p.sem("st")
    pv = p.sem("pv")
    rbs = p.sem("rbs")

    ident_sb, tri_sb = io["ident_sb"], io["tri_sb"]
    onesc_sb, zeroc_sb = io["onescol_sb"], io["zerob_sb"]
    onesr_sb = io["onesrowf_sb"]
    csh_sb = io["csh0_sb"] if layer == 0 else io["csh1_sb"]
    cld = io["constld"]

    qh = [p.sb(f"q{i}", [128, S], BF16) for i in range(2)]
    kh = [p.sb(f"k{i}", [128, S], BF16) for i in range(2)]
    vh = [p.sb(f"v{i}", [128, S], BF16) for i in range(2)]
    vtm = [p.sb(f"vtm{i}", [128, S], BF16) for i in range(2)]
    pt = [p.sb(f"pt{i}", [128, 512], BF16) for i in range(4)]
    lrow = [p.sb(f"lrow{i}", [1, 512], F32) for i in range(2)]
    rb_sb = [p.sb(f"rb{i}", [128, 512], F32) for i in range(2)]
    ctx_sb = p.sb("ctx", [128, S], BF16)

    ps_cl = []

    def spsum(name, shape, dt):
        h, cl = p.scoped_psum(name, shape, dt)
        ps_cl.append(cl)
        return h

    ps_s = [spsum(f"pss{i}", [128, 512], F32) for i in range(4)]
    ps_o = [spsum(f"pso{i}", [128, 512], F32) for i in range(2)]
    ps_l = spsum("psl", [33, 512], F32)
    ps_tr = spsum("pstr", [128, 512], BF16)

    ldh_tok = [0] * 4
    pv_end = [0] * 4
    vtm_ve = [0] * 4
    st_end = [0] * 4
    exp_tok = {}
    pvp_tok = {}
    veo_tok = {}
    scrb_tok = {}
    pieces_sbi = 0

    for (sm, val) in gates:
        p.g.wait(sm, val)

    for j in range(4):
        hb = j % 2
        kv = j if layer == 0 else 0
        if j >= 2:
            p.g.wait(pv, pv_end[j - 2])
        l1 = p.g.dma_start(out=qh[hb][:], in_=qT[128 * j:128 * (j + 1), :]).inc(ldh, 16)
        l2 = p.g.dma_start(out=kh[hb][:], in_=kT[128 * kv:128 * (kv + 1), :]).inc(ldh, 16)
        l3 = p.g.dma_start(out=vh[hb][:], in_=vT[128 * kv:128 * (kv + 1), :]).inc(ldh, 16)
        ldh_tok[j] = l3
        p.t.wait(ldh, l3)
        if j == 0:
            p.t.wait(cld[0], cld[1])
        if j >= 2:
            p.t.wait(pv, pv_end[j - 2])
        ev_pr = 0
        for grp in range(16):
            p.t.wait(ve, ev_pr)
            tok = None
            for q in range(4):
                ktile = grp * 4 + q
                tok = p.t.transpose(ps_tr[:, 128 * q:128 * (q + 1)],
                                    vh[hb][:, 128 * ktile:128 * (ktile + 1)], ident_sb[:])
            mm_t = tok.inc(mm, 1)
            p.v.wait(mm, mm_t)
            ev_pr = p.v.tensor_copy(vtm[hb][:, 512 * grp:512 * (grp + 1)], ps_tr[:]).inc(ve, 1)
        vtm_ve[j] = ev_pr
        p.t.wait(ve, ev_pr)
        p.t.wait(ldh, l1)

        for qc in range(16):
            pp = qc % 2
            pieces = attn_pieces(layer, qc)
            if (j, qc) != (0, 0):
                prev = veo_tok.get((j, qc - 2)) if qc >= 2 else veo_tok.get((j - 1, 14 + qc))
                if prev:
                    p.t.wait(ve, prev + 1)
            p.t.matmul(ps_o[pp][:, 0:512], zeroc_sb[:], kh[hb][:, 0:512],
                       start=True, stop=False)
            p.t.matmul(ps_l[32 * pp:32 * pp + 1, 0:512], zeroc_sb[:, 0:1],
                       kh[hb][:, 0:512], start=True, stop=False)
            for (kt, kpart, c0, c1_, diag) in pieces:
                sb = pieces_sbi % 4
                pieces_sbi += 1
                if kpart == "full":
                    lhs = kh[hb][:, 128 * kt:128 * (kt + 1)]
                    prow = slice(0, 128)
                else:
                    lhs = kh[hb][:, 128 * kt + 64:128 * (kt + 1)]
                    prow = slice(64, 128)
                if pieces_sbi > 3:
                    p.t.wait(sc, exp_tok[pieces_sbi - 1 - 3])
                s_t = p.t.matmul(ps_s[sb][prow, c0:c1_], lhs,
                                 qh[hb][:, 512 * qc + c0:512 * qc + c1_],
                                 start=True, stop=True).inc(mm, 1)
                p.s.wait(mm, s_t)
                if pieces_sbi > 3:
                    p.s.wait(pv, pvp_tok[pieces_sbi - 1 - 3])
                e_t = p.s.activation(out=pt[sb][prow, c0:c1_], in_=ps_s[sb][prow, c0:c1_],
                                     func=AF.Exp, bias=csh_sb[prow, 0:1]).inc(sc, 1)
                exp_tok[pieces_sbi - 1] = e_t
                m_t = None
                if diag is not None:
                    p.v.wait(sc, e_t)
                    m_t = p.v.tensor_mul(pt[sb][:, diag:diag + 128],
                                         pt[sb][:, diag:diag + 128], tri_sb[:]).inc(ve, 1)
                if m_t is not None:
                    p.t.wait(ve, m_t)
                else:
                    p.t.wait(sc, e_t)
                vlhs = vtm[hb][prow, 128 * kt:128 * (kt + 1)]
                p.t.matmul(ps_o[pp][:, c0:c1_], vlhs, pt[sb][prow, c0:c1_],
                           start=False, stop=False)
                pvp = p.t.matmul(ps_l[32 * pp:32 * pp + 1, c0:c1_], onesc_sb[prow, :],
                                 pt[sb][prow, c0:c1_],
                                 start=False, stop=False).inc(pv, 1)
                pvp_tok[pieces_sbi - 1] = pvp
            mm_all = pv.n
            p.v.wait(pv, mm_all)
            p.v.reciprocal(lrow[pp][:], ps_l[32 * pp:32 * pp + 1, :]).inc(ve, 1)
            ve_l = p.v.tensor_copy(rb_sb[pp][0:1, 0:4],
                                   lrow[pp][0:1, 0:4]).inc(ve, 1)
            prev_ctx = veo_tok.get((j, qc - 2)) if qc >= 2 else veo_tok.get((j - 1, 14 + qc))
            if prev_ctx:
                p.sy.wait(ve, prev_ctx + 1)
            p.sy.wait(ve, ve_l)
            p.sy.dma_start(out=AP(lrowd.tensor, lrowd.offset + 512 * pp,
                                  [[512, 1], [1, 512]]),
                           in_=lrow[pp][:]).inc(rbs, 16)
            rb_t = p.sy.dma_start(out=rb_sb[pp][:],
                                  in_=bcast_row(lrowd, pp, 0, 512)).inc(rbs, 16)
            p.v.wait(rbs, rb_t)
            if qc == 0 and j > 0:
                p.v.wait(st, st_end[j - 1])
            veo_tok[(j, qc)] = p.v.tensor_mul(ctx_sb[:, 512 * qc:512 * (qc + 1)],
                                              ps_o[pp][:], rb_sb[pp][:]).inc(ve, 1)
        pv_end[j] = pv.n
        if j == 3:
            p.v.tensor_copy(lrow[0][0:1, 0:8], rb_sb[0][0:1, 0:8]).inc(ve, 1)
        p.sy.wait(ve, veo_tok[(j, 15)] + 1)
        for r in range(NCORE):
            p.sy.dma_start(out=ctxa[512 * r + 128 * j: 512 * r + 128 * (j + 1), :],
                           in_=ctx_sb[:, T * r:T * (r + 1)]).inc(st, 16)
        st_end[j] = st.n
    p.sy.wait(st, st.n)
    for cl in reversed(ps_cl):
        cl()
    p.pop_scope()
    return st, st.n



def emit_oproj_ring(p, layer, wot, qst_gate, n_ungated=3):
    p.prefix = f"o{layer}_"
    bufs, closers = [], []
    for i in range(3):
        h, cl = p.scoped_sb(f"wo{i}", [128, D], BF16)
        bufs.append(h)
        closers.append(cl)
    ring = {
        "bufs": bufs,
        "closers": closers,
        "wl": p.sem("wl"),
        "op_mm": p.sem("op_mm"),
        "wot": wot,
        "qst_gate": qst_gate,
        "n_ungated": n_ungated,
    }
    return ring


def emit_ring_loads(p, layer, ring, first, last):
    p.prefix = f"o{layer}_"
    for ot in range(first, last):
        if ot < ring["n_ungated"]:
            sm, val = ring["qst_gate"]
            p.g.wait(sm, val)
        else:
            p.g.wait(ring["op_mm"], ot - 2)
        p.g.dma_start(out=ring["bufs"][ot % 3][:],
                      in_=rowsap(ring["wot"], 128 * ot)).inc(ring["wl"], 16)


def emit_oproj(p, layer, io, ring, ccgate):
    p.prefix = f"o{layer}_"
    p.push_scope()
    ctx_sl, obias = io["ctx_sl"], io["obias"]
    gT, xT_in, xoT = io["gT"], io["x"], io["xo"]

    ld = p.sem("ld")
    ld2 = p.sem("ld2")
    gx = p.sem("gx")
    mm = ring["op_mm"]
    ve = p.sem("ve")
    sto = p.sem("sto")

    ctx_sb = p.sb("ctx", [128, 32 * T], BF16)
    ob_sb = p.sb("ob", [128, 32], F32)
    g_sb = [p.sb(f"g{i}", [128, T], BF16) for i in range(3)]
    x_sb = [p.sb(f"x{i}", [128, T], F32) for i in range(3)]
    tmp_sb = [p.sb(f"tmp{i}", [128, T], F32) for i in range(2)]
    xo_sb = [p.sb(f"xo{i}", [128, T], F32) for i in range(2)]

    ps_cl = []
    ps_s = []
    for i in range(3):
        h, cl = p.scoped_psum(f"pss{i}", [128, T], F32)
        ps_s.append(h)
        ps_cl.append(cl)
    ps_junk, cl_junk = p.scoped_psum("psjunk", [1, 512], F32)
    ps_cl.append(cl_junk)

    wl = ring["wl"]

    p.sy.wait(ccgate[0], ccgate[1])
    p.s.wait(ccgate[0], ccgate[1])
    t0 = t0b = None
    for i, ct0 in enumerate(range(0, 32, 4)):
        if i % 2 == 0:
            t0 = p.sy.dma_start(out=ctx_sb[:, ct0 * T:(ct0 + 4) * T],
                                in_=pview(ctx_sl, 128 * ct0, 512, 0, T)).inc(ld, 16)
        else:
            t0b = p.s.dma_start(out=ctx_sb[:, ct0 * T:(ct0 + 4) * T],
                                in_=pview(ctx_sl, 128 * ct0, 512, 0, T)).inc(ld2, 16)
    t2 = p.sy.dma_start(out=ob_sb[:], in_=obias).inc(ld, 16)

    gstg = io.get("g_gate")
    gx_tok = [0] * 3
    ve_tok = [0] * 32
    st_tok = [0, 0]

    def load_gx(ot):
        b = ot % 3
        if ot < 3 and gstg is not None:
            p.sy.wait(gstg[0], gstg[1])
        gx_tok[b] = p.sy.dma_start(out=g_sb[b][:], in_=rowsap(gT, 128 * ot)).inc(gx, 16)
        gx_tok[b] = p.sy.dma_start(out=x_sb[b][:], in_=rowsap(xT_in, 128 * ot)).inc(gx, 16)

    load_gx(0)
    load_gx(1)
    load_gx(2)
    for ot in range(32):
        b = ot % 3
        b2 = ot % 2
        p.t.wait(wl, 16 * (ot + 1))
        if ot == 0:
            p.t.wait(ld, t0)
            p.t.wait(ld2, t0b)
        if ot >= 3:
            p.t.wait(ve, ve_tok[ot - 2])
        tok = None
        for ct in range(32):
            for th in range(2):
                tok = p.t.matmul(ps_s[b][:, 512 * th:512 * (th + 1)],
                                 ring["bufs"][ot % 3][:, ct * 128:(ct + 1) * 128],
                                 ctx_sb[:, ct * T + 512 * th: ct * T + 512 * (th + 1)],
                                 start=(ct == 0), stop=(ct == 31))
        mm_t = tok.inc(mm, 1)
        p.v.wait(mm, mm_t + 1 if ot < 31 else 33)
        p.v.wait(gx, gx_tok[b])
        if ot == 0:
            p.v.wait(ld, t2)
        p.v.wait(sto, st_tok[b2])
        p.v.scalar_tensor_tensor(out=tmp_sb[b2][:], in0=ps_s[b][:],
                                 scalar=ob_sb[:, ot:ot + 1], in1=g_sb[b][:],
                                 op0=ALU.add, op1=ALU.mult)
        ve_tok[ot] = p.v.tensor_add(xo_sb[b2][:], tmp_sb[b2][:], x_sb[b][:]).inc(ve, 1)
        if ot == 31:
            p.v.tensor_copy(tmp_sb[0][0:1, 0:8], ob_sb[0:1, 0:8]).inc(ve, 1)
        p.sy.wait(ve, ve_tok[ot] + 1)
        st_tok[b2] = p.sy.dma_start(out=rowsap(xoT, 128 * ot), in_=xo_sb[b2][:]).inc(sto, 16)
        if ot + 3 < 32:
            load_gx(ot + 3)
    p.t.matmul(ps_junk[0:1, :], ring["bufs"][31 % 3][:, 0:1], ctx_sb[:, 0:512],
               start=True, stop=True).inc(mm, 1)
    p.sy.wait(sto, sto.n)
    for cl in reversed(ps_cl):
        cl()
    p.pop_scope()
    return sto, sto.n



def _rope_tables(theta, scale):
    j = np.arange(64, dtype=np.float64)
    inv = theta ** (-2.0 * j / 128.0)
    t = np.arange(S, dtype=np.float64)
    f = np.outer(inv, t)
    return (np.cos(f) * scale).astype(np.float32), (np.sin(f) * scale).astype(np.float32)


def host_prep(inputs):
    d = {}
    d["ident"] = np.eye(128, dtype=np.float32).astype(BF)
    kk, qq = np.meshgrid(np.arange(128), np.arange(128), indexing="ij")
    d["tri"] = (kk <= qq).astype(np.float32).astype(BF)
    d["zerob"] = np.zeros((128, 128), dtype=np.float32).astype(BF)
    d["onescol"] = np.ones((128, 1), dtype=np.float32).astype(BF)
    d["onescolf"] = np.ones((128, 1), dtype=np.float32)
    d["onesrow"] = np.ones((1, 128), dtype=np.float32)
    d["ones8"] = np.ones((8, 1), dtype=np.float32)
    rep = np.zeros((8, 128), dtype=np.float32)
    for h in range(8):
        rep[h, 16 * h:16 * h + 16] = 1.0
    d["rep"] = rep
    d["shiftv"] = (4 * (np.arange(128) % 8)).astype(np.int32).reshape(128, 1)
    e32 = np.zeros((128, 32 * 32), dtype=np.float32)
    for t in range(32):
        e32[:, 32 * t + t] = 1.0
    d["e32"] = e32.astype(BF)
    d["csh0"] = np.full((128, 1), -C_SHIFT[0], dtype=np.float32)
    d["csh1"] = np.full((128, 1), -C_SHIFT[1], dtype=np.float32)

    cq0, sq0 = _rope_tables(10000.0, 1.0 / math.sqrt(HD))
    ck0, sk0 = _rope_tables(10000.0, 1.0)
    d["rope0"] = (cq0, sq0, ck0, sk0)
    cq1, sq1 = _rope_tables(1000000.0, 1.0 / HD)
    ck1, sk1 = _rope_tables(1000000.0, 1.0)
    d["rope1"] = (cq1, sq1, ck1, sk1)

    perm = np.concatenate([np.arange(0, 128, 2), np.arange(1, 128, 2)])
    wq = np.asarray(inputs["L0_attn_wq"])
    wk = np.asarray(inputs["L0_attn_wk"])
    wv = np.asarray(inputs["L0_attn_wv"])
    d["w0"] = []
    for c in range(NCORE):
        cols = []
        for w, permute in ((wq, True), (wk, True), (wv, False)):
            for j in range(4):
                h = 4 * c + j
                idx = h * 128 + (perm if permute else np.arange(128))
                cols.append(w[:, idx])
        d["w0"].append(np.ascontiguousarray(np.concatenate(cols, axis=1), dtype=np.float32))

    wqkv = np.asarray(inputs["L1_attn_w_qkv"])
    bqkv = np.asarray(inputs["L1_attn_b_qkv"])
    d["w1"] = []
    d["b1"] = []
    for c in range(NCORE):
        rows = np.arange(768 * c, 768 * (c + 1))
        d["w1"].append(np.ascontiguousarray(wqkv[rows].T, dtype=np.float32))
        d["b1"].append(np.ascontiguousarray(bqkv[rows].reshape(6, 128).T, dtype=np.float32))

    d["wrt"] = np.ascontiguousarray(np.asarray(inputs["L0_ffn_Wr"]).T, dtype=np.float32)
    d["at"] = np.ascontiguousarray(np.asarray(inputs["L0_ffn_A"]).reshape(128, D).T,
                                   dtype=np.float32)
    d["bt"] = np.ascontiguousarray(
        np.asarray(inputs["L0_ffn_B"]).transpose(0, 2, 1).reshape(128, D), dtype=np.float32)
    d["br"] = np.asarray(inputs["L0_ffn_br"], dtype=np.float32).reshape(8, 1)

    qw = np.asarray(inputs["L1_ffn_qweight"])
    d["qwrep"] = np.ascontiguousarray(np.repeat(qw, 8, axis=0))
    d["qzeros"] = np.asarray(inputs["L1_ffn_qzeros"])
    d["scales"] = np.asarray(inputs["L1_ffn_scales"], dtype=np.float32)
    d["fbias"] = np.ascontiguousarray(
        np.asarray(inputs["L1_ffn_bias"], dtype=np.float32).reshape(32, 128).T)

    wo0 = np.asarray(inputs["L0_attn_wo"], dtype=np.float32)
    wo1 = np.ascontiguousarray(np.asarray(inputs["L1_attn_w_dense"]).T, dtype=np.float32)
    d["wo0t"] = np.ascontiguousarray(
        wo0.reshape(32, 128, 32, 128).transpose(2, 1, 0, 3).reshape(D, D))
    d["wo1t"] = np.ascontiguousarray(
        wo1.reshape(32, 128, 32, 128).transpose(2, 1, 0, 3).reshape(D, D))
    d["ob0"] = np.zeros((128, 32), dtype=np.float32)
    d["ob1"] = np.ascontiguousarray(
        np.asarray(inputs["L1_attn_b_dense"], dtype=np.float32).reshape(32, 128).T)

    xT = np.asarray(inputs["L0_attn_x"], dtype=np.float32).reshape(S, D).T
    d["x"] = [np.ascontiguousarray(xT[:, T * c:T * (c + 1)]) for c in range(NCORE)]
    return d


CONST_KEYS = ("ident", "tri", "zerob", "onescol", "onescolf", "onesrow", "ones8", "rep",
              "shiftv", "e32", "csh0", "csh1")



def build_fused():
    p = P("mega")
    x_in = p.din("x", [D, T], F32)
    w0 = p.din("w0", [D, 1536], F32)
    w1 = p.din("w1", [D, 768], F32)
    bqkv1 = p.din("bqkv1", [128, 6], F32)
    wo0t = p.din("wo0t", [D, D], F32)
    wo1t = p.din("wo1t", [D, D], F32)
    ob0 = p.din("ob0", [128, 32], F32)
    ob1 = p.din("ob1", [128, 32], F32)
    wrt = p.din("wrt", [D, 8], F32)
    at = p.din("at", [D, 128], F32)
    bt = p.din("bt", [128, D], F32)
    br = p.din("br", [8, 1], F32)
    qwrep = p.din("qwrep", [D, D], I32)
    qzeros = p.din("qzeros", [32, 512], I32)
    scales = p.din("scales", [32, D], F32)
    fbias = p.din("fbias", [128, 32], F32)
    ropes = {}
    for l in (0, 1):
        for nm in ("cq", "sq", "ck", "sk"):
            ropes[(l, nm)] = p.din(f"{nm}{l}", [64, S], F32)
    xo_out = p.dout("xo", [D, T], F32)

    p.prefix = "c_"
    cled = p.sem("cled")
    io_c = {}
    for nm, shape, dt in (("ident", [128, 128], BF16), ("tri", [128, 128], BF16),
                          ("zerob", [128, 128], BF16), ("onescol", [128, 1], BF16),
                          ("onescolf", [128, 1], F32), ("onesrowf", [1, 128], F32),
                          ("csh0", [128, 1], F32), ("csh1", [128, 1], F32)):
        dnm = "onesrow" if nm == "onesrowf" else nm
        din = p.din(dnm, shape, dt)
        sbuf = p.sb(nm, shape, dt)
        p.sy.dma_start(out=sbuf[:], in_=din).inc(cled, 16)
        io_c[nm + "_sb"] = sbuf
        io_c[nm + "_din"] = din
    cld = (cled, cled.n)
    io_c["constld"] = cld

    ones8 = p.din("ones8", [8, 1], F32)
    rep = p.din("rep", [8, 128], F32)
    shiftv = p.din("shiftv", [128, 1], I32)
    e32 = p.din("e32", [128, 32 * 32], BF16)

    def mk(name, shape, dt):
        return p.dout(name, shape, dt) if name in TAPS else p.dram(name, shape, dt)

    xnT = [mk(f"i_xnT{l}", [D, T], BF16) for l in (0, 1)]
    xnTf = [p.dram(f"i_xnTf{l}", [NCORE * D, T], BF16) for l in (0, 1)]
    gT = [mk(f"i_gT{l}", [D, T], BF16) for l in (0, 1)]
    qTd = [mk(f"i_qT{l}", [512, S], BF16) for l in (0, 1)]
    kTd = [mk("i_kT0", [512, S], BF16), mk("i_kT1", [128, S], BF16)]
    vTd = [mk("i_vT0", [512, S], BF16), mk("i_vT1", [128, S], BF16)]
    ctxa = [mk(f"i_ctxa{l}", [NCORE * 512, T], BF16) for l in (0, 1)]
    ctxsl = [p.dram(f"i_ctxsl{l}", [D, T], BF16) for l in (0, 1)]
    x1d = mk("i_x1", [D, T], F32)

    cc = [p.sem(f"cc{i}") for i in range(4)]

    prev_gate = ()
    for l in (0, 1):
        xin_l = x_in if l == 0 else x1d
        p.push_scope()
        xnT_sb = p.sb(f"xnTsb{l}", [128, 32 * T], BF16)
        lnio = {"onescolf": io_c["onescolf_sb"], "onesrow": io_c["onesrowf_sb"],
                "constld": cld}
        if f"dbg_xn{l}" in TAPS:
            lnio["dbg_xn"] = p.dout(f"dbg_xn{l}", [D, T], BF16)
        (stx, stx_n), xn_gate = emit_ln(p, l, xin_l, xnT_sb, xnT[l], lnio,
                                        gates=prev_gate)
        if l == 0:
            f_io = {"wrt": wrt, "at": at, "bt": bt, "br": br, "rep": rep,
                    "ones8": ones8, "onesrow": io_c["onesrowf_sb"]}
            if "dbg_ffn0" in TAPS:
                f_io["dbg"] = {"elog": p.dout("dbg_elog", [8, T], F32),
                               "scn": p.dout("dbg_scn", [128, T], F32),
                               "slow": p.dout("dbg_slow", [128, T], BF16)}
            stg, stg_n = emit_ffn0(p, xnT_sb, xn_gate, gT[l], f_io, ldgates=prev_gate)
        else:
            f_io = {"qwrep": qwrep, "qzeros": qzeros, "scales": scales,
                    "fbias": fbias, "shiftv": shiftv, "e32": e32}
            stg, stg_n = emit_ffn1(p, xnT_sb, xn_gate, gT[l], f_io, ldgates=prev_gate)
        p.prefix = f"c{l}_"
        p.g.wait(stx, stx_n)
        p.g.wait(stg, stg_n)
        p.g.collective_compute(
            "AllGather", ALU.bypass, replica_groups=[list(range(NCORE))],
            ins=[xnT[l]],
            outs=[xnTf[l]],
        ).inc(cc[2 * l], 1)
        p.pop_scope()

        qio = {"w": w0 if l == 0 else w1}
        for nm in ("cq", "sq", "ck", "sk"):
            qio[nm] = ropes[(l, nm)]
        if l == 1:
            qio["bqkv"] = bqkv1
        outs = {"qT": qTd[l], "kT": kTd[l], "vT": vTd[l]}
        qst, qst_n = emit_qkv(p, l, xnTf[l], outs, qio,
                              wgates=((stx, stx_n), (stg, stg_n)),
                              ccgate=(cc[2 * l], 1))

        ring = emit_oproj_ring(p, l, wo0t if l == 0 else wo1t, (qst, qst_n))

        aio = dict(io_c)
        aio.update({"qT": qTd[l], "kT": kTd[l], "vT": vTd[l], "ctxa": ctxa[l]})
        ast, ast_n = emit_attn(p, l, aio, gates=((qst, qst_n),))

        emit_ring_loads(p, l, ring, 0, 3)
        p.prefix = f"d{l}_"
        p.g.wait(ast, ast_n)
        p.g.collective_compute(
            "AllToAll", ALU.bypass, replica_groups=[list(range(NCORE))],
            ins=[ctxa[l]], outs=[ctxsl[l]],
        ).inc(cc[2 * l + 1], 1)
        emit_ring_loads(p, l, ring, 3, 32)

        oio = {"ctx_sl": ctxsl[l], "obias": ob0 if l == 0 else ob1, "gT": gT[l],
               "x": xin_l, "xo": x1d if l == 0 else xo_out,
               "g_gate": (stg, stg_n)}
        ost, ost_n = emit_oproj(p, l, oio, ring, ccgate=(cc[2 * l + 1], 1))
        for cl in reversed(ring["closers"]):
            cl()
        prev_gate = ((ost, ost_n),)
    return p.emit()


def kernel(**inputs):
    d = host_prep(inputs)
    cores = list(range(NCORE))
    with allow_ici_all_to_all():
        nc = build_fused()
        maps = []
        for c in cores:
            m = {k: d[k] for k in CONST_KEYS}
            m.update({"x": d["x"][c], "w0": d["w0"][c], "w1": d["w1"][c],
                      "bqkv1": d["b1"][c], "wo0t": d["wo0t"], "wo1t": d["wo1t"],
                      "ob0": d["ob0"], "ob1": d["ob1"], "wrt": d["wrt"], "at": d["at"],
                      "bt": d["bt"], "br": d["br"], "qwrep": d["qwrep"],
                      "qzeros": d["qzeros"], "scales": d["scales"], "fbias": d["fbias"]})
            for l in (0, 1):
                cq, sq, ck, sk = d[f"rope{l}"]
                m.update({f"cq{l}": cq, f"sq{l}": sq, f"ck{l}": ck, f"sk{l}": sk})
            maps.append(m)
        res = run_bass_kernel_spmd(nc, maps, core_ids=cores)
    y = np.concatenate([np.asarray(res.results[c]["xo"]) for c in cores], axis=1)
    return np.ascontiguousarray(y.T.reshape(1, S, D), dtype=np.float32)


# version 19 (final)
# speedup vs baseline: 1.0744x; 1.0111x over previous
import math
from contextlib import ExitStack

import numpy as np
import ml_dtypes

import concourse.bass as bass
import concourse.mybir as mybir
from concourse.ap import AP
from concourse.bass_utils import run_bass_kernel_spmd
from concourse.aot_env import allow_ici_all_to_all

F32 = mybir.dt.float32
BF16 = mybir.dt.bfloat16
I32 = mybir.dt.int32
AF = mybir.ActivationFunctionType
ALU = mybir.AluOpType

NCORE = 8
S = 8192
D = 4096
T = S // NCORE
HD = 128
C_SHIFT = {0: 20.0, 1: 5.0}
EPS = 1e-5

BF = ml_dtypes.bfloat16
TAPS = ()



class Sem:
    __slots__ = ("h", "n")

    def __init__(self, h):
        self.h = h
        self.n = 0


class OpTok:
    __slots__ = ("incs",)

    def __init__(self):
        self.incs = []

    def inc(self, sem, k=1):
        sem.n += k
        self.incs.append((sem, k))
        return sem.n


class EngRec:
    def __init__(self, prog, eng):
        self._p = prog
        self._e = eng

    def wait(self, sem, val):
        if val > 0:
            self._p.ops[self._e].append(("wait_ge", (sem.h, val), {}, OpTok()))

    def __getattr__(self, opname):
        p, e = self._p, self._e

        def call(*a, **k):
            tok = OpTok()
            p.ops[e].append((opname, a, k, tok))
            return tok
        return call


class P:

    def __init__(self, name):
        self.name = name
        self.nc = bass.Bass("TRN2", target_bir_lowering=False)
        self.ctx = ExitStack()
        self.ops = {e: [] for e in ("sync", "tensor", "vector", "scalar", "gpsimd")}
        self.t = EngRec(self, "tensor")
        self.v = EngRec(self, "vector")
        self.s = EngRec(self, "scalar")
        self.g = EngRec(self, "gpsimd")
        self.sy = EngRec(self, "sync")
        self.prefix = ""
        self._scopes = []

    def sem(self, name):
        return Sem(self.ctx.enter_context(self.nc.semaphore(self.prefix + name)))

    def _alloc(self, cm):
        h = cm.__enter__()
        if self._scopes:
            self._scopes[-1].append(cm)
        else:
            self.ctx.callback(lambda c=cm: None)
        return h

    def sb(self, name, shape, dt):
        return self._alloc(self.nc.sbuf_tensor(self.prefix + name, shape, dt))

    def psum(self, name, shape, dt):
        return self._alloc(self.nc.psum_tensor(self.prefix + name, shape, dt))

    def scoped_sb(self, name, shape, dt):
        cm = self.nc.sbuf_tensor(self.prefix + name, shape, dt)
        h = cm.__enter__()
        return h, (lambda: cm.__exit__(None, None, None))

    def scoped_psum(self, name, shape, dt):
        cm = self.nc.psum_tensor(self.prefix + name, shape, dt)
        h = cm.__enter__()
        return h, (lambda: cm.__exit__(None, None, None))

    def push_scope(self):
        self._scopes.append([])

    def pop_scope(self):
        for cm in reversed(self._scopes.pop()):
            cm.__exit__(None, None, None)

    def din(self, name, shape, dt):
        return self.nc.dram_tensor(name, shape, dt, kind="ExternalInput").ap()

    def dout(self, name, shape, dt):
        return self.nc.dram_tensor(name, shape, dt, kind="ExternalOutput").ap()

    def dram(self, name, shape, dt, addr_space="Local"):
        return self.nc.dram_tensor(name, shape, dt, kind="Internal",
                                   addr_space=addr_space).ap()

    def emit(self):
        blk = self.ctx.enter_context(self.nc.Block())

        def runner(eng_name):
            def run(eng):
                for (opname, a, k, tok) in self.ops[eng_name]:
                    inst = getattr(eng, opname)(*a, **k)
                    for (sem, kk) in tok.incs:
                        inst.then_inc(sem.h, kk)
            return run

        for eng_name, method in (("sync", blk.sync), ("tensor", blk.tensor),
                                 ("vector", blk.vector), ("scalar", blk.scalar),
                                 ("gpsimd", blk.gpsimd)):
            if self.ops[eng_name]:
                method(runner(eng_name))
        self.ctx.close()
        return self.nc


def pview(dram_ap, r0, nrows, c0, ncols, p=128):
    t = dram_ap.tensor
    C = t.shape[-1]
    off = dram_ap.offset + r0 * C + c0
    return AP(t, off, [[C, p], [p * C, nrows // p], [1, ncols]])


def gload(p, dst_sb, dram_ap, r0, nrows, c0, ncols, sem, dst_c0=0, tiles_per_dma=4,
          eng=None):
    eng = eng or p.g
    nt = nrows // 128
    for t0 in range(0, nt, tiles_per_dma):
        tcnt = min(tiles_per_dma, nt - t0)
        src = pview(dram_ap, r0 + 128 * t0, 128 * tcnt, c0, ncols)
        dst = dst_sb[:, dst_c0 + t0 * ncols: dst_c0 + (t0 + tcnt) * ncols]
        eng.dma_start(out=dst, in_=src).inc(sem, 16)
    return sem.n


def bcast_row(dram_ap, row, col0, ncols, nparts=128):
    t = dram_ap.tensor
    rowlen = t.shape[-1]
    off = dram_ap.offset + row * rowlen + col0
    return AP(t, off, [[0, nparts], [1, ncols]])


def rowsap(dram_ap, r0, nrows=128):
    t = dram_ap.tensor
    C = t.shape[-1]
    return AP(t, dram_ap.offset + r0 * C, [[C, nrows], [1, C]])



def emit_ln(p, layer, xT_in, xnT_sb, xnT_out, io, gates=()):
    p.prefix = f"ln{layer}_"
    ld = p.sem("ld")
    mm = p.sem("mm")
    ve = p.sem("ve")
    sc = p.sem("sc")
    stx = p.sem("stx")

    x_sb = [p.sb(f"x{i}", [128, T], F32) for i in range(3)]
    xq_sb = [p.sb(f"xq{i}", [128, T], F32) for i in range(2)]
    m_sb = p.sb("m", [1, T], F32)
    e2_sb = p.sb("e2", [1, T], F32)
    var_sb = p.sb("var", [1, T], F32)
    a_sb = p.sb("a", [1, T], F32)
    b_sb = p.sb("b", [1, T], F32)
    ab_sb = p.sb("ab", [128, T], F32)
    bb_sb = p.sb("bb", [128, T], F32)
    tmp_sb = p.sb("tmp", [128, T], F32)

    ps_sx, cl1 = p.scoped_psum("ps_sx", [1, T], F32)
    ps_sq, cl0 = p.scoped_psum("ps_sq", [1, T], F32)
    ps_a, cl2 = p.scoped_psum("ps_a", [128, T], F32)
    ps_b, cl3 = p.scoped_psum("ps_b", [128, T], F32)

    onescolf, onesrow = io["onescolf"], io["onesrow"]
    cld = io["constld"]

    mm_tok = [0] * 32
    ld_tok = [0] * 32
    for ct in range(32):
        b = ct % 3
        for (sm, val) in gates:
            if ct < 3:
                p.sy.wait(sm, val)
        if ct >= 3:
            p.sy.wait(mm, mm_tok[ct - 3])
        ld_tok[ct] = p.sy.dma_start(out=x_sb[b][:],
                                    in_=rowsap(xT_in, 128 * ct)).inc(ld, 16)
        b2 = ct % 2
        p.v.wait(ld, ld_tok[ct])
        if ct >= 2:
            p.v.wait(mm, mm_tok[ct - 2])
        ve_sq = p.v.tensor_mul(xq_sb[b2][:], x_sb[b][:], x_sb[b][:]).inc(ve, 1)
        p.t.wait(ld, ld_tok[ct])
        p.t.wait(ve, ve_sq)
        if ct == 0:
            p.t.wait(cld[0], cld[1])
        tok = None
        for th in range(2):
            p.t.matmul(ps_sx[0:1, th * 512:(th + 1) * 512], onescolf[:],
                       x_sb[b][:, th * 512:(th + 1) * 512],
                       start=(ct == 0), stop=(ct == 31))
            tok = p.t.matmul(ps_sq[0:1, th * 512:(th + 1) * 512], onescolf[:],
                             xq_sb[b2][:, th * 512:(th + 1) * 512],
                             start=(ct == 0), stop=(ct == 31))
        mm_tok[ct] = tok.inc(mm, 1)
    dmy = p.t.matmul(ps_a[0:1, 0:512], onescolf[:], x_sb[31 % 3][:, 0:512],
                     start=True, stop=True).inc(mm, 1)
    p.s.wait(mm, dmy)
    p.s.copy(m_sb[:], ps_sx[:])
    sc_ev = p.s.copy(e2_sb[:], ps_sq[:]).inc(sc, 1)

    p.v.wait(sc, sc_ev)
    p.v.tensor_scalar(out=m_sb[:], in0=m_sb[:], scalar1=1.0 / D, scalar2=None,
                      op0=ALU.mult)
    p.v.tensor_scalar(out=e2_sb[:], in0=e2_sb[:], scalar1=1.0 / D, scalar2=None,
                      op0=ALU.mult)
    p.v.tensor_mul(var_sb[:], m_sb[:], m_sb[:])
    p.v.tensor_sub(var_sb[:], e2_sb[:], var_sb[:])
    ve_v = p.v.tensor_scalar(out=var_sb[:], in0=var_sb[:], scalar1=EPS, scalar2=None,
                             op0=ALU.add).inc(ve, 1)
    p.s.wait(ve, ve_v)
    p.s.activation(out=var_sb[:], in_=var_sb[:], func=AF.Ln)
    sc_a = p.s.activation(out=a_sb[:], in_=var_sb[:], func=AF.Exp, scale=-0.5).inc(sc, 1)
    p.v.wait(sc, sc_a)
    ve_b = p.v.scalar_tensor_tensor(out=b_sb[:], in0=m_sb[:], scalar=-1.0, in1=a_sb[:],
                                    op0=ALU.mult, op1=ALU.mult).inc(ve, 1)
    p.t.wait(ve, ve_b)
    for th in range(2):
        p.t.matmul(ps_a[:, th * 512:(th + 1) * 512], onesrow[:],
                   a_sb[0:1, th * 512:(th + 1) * 512], start=True, stop=True)
        tok = p.t.matmul(ps_b[:, th * 512:(th + 1) * 512], onesrow[:],
                         b_sb[0:1, th * 512:(th + 1) * 512], start=True, stop=True)
    tok.inc(mm, 1)
    dmy2 = p.t.matmul(ps_sx[0:1, 0:512], onescolf[:], x_sb[31 % 3][:, 0:512],
                      start=True, stop=True).inc(mm, 1)
    p.v.wait(mm, dmy2)
    p.v.tensor_copy(ab_sb[:], ps_a[:])
    ve_bc = p.v.tensor_copy(bb_sb[:], ps_b[:]).inc(ve, 1)

    xn_done = [0] * 32
    ld2_tok = [0] * 32

    def load2(ct):
        b = ct % 3
        if ct < 3:
            p.sy.wait(mm, mm_tok[31])
        else:
            p.sy.wait(ve, xn_done[ct - 3])
        ld2_tok[ct] = p.sy.dma_start(out=x_sb[b][:],
                                     in_=rowsap(xT_in, 128 * ct)).inc(ld, 16)

    load2(0)
    for ct in range(32):
        b = ct % 3
        if ct + 1 < 32:
            load2(ct + 1)
        p.v.wait(ld, ld2_tok[ct])
        if ct == 0:
            p.v.wait(ve, ve_bc)
        p.v.tensor_mul(tmp_sb[:], x_sb[b][:], ab_sb[:])
        xn_done[ct] = p.v.tensor_add(xnT_sb[:, ct * T:(ct + 1) * T], tmp_sb[:],
                                     bb_sb[:]).inc(ve, 1)
        if ct % 4 == 3:
            g4 = ct // 4
            p.sy.wait(ve, xn_done[ct] + 1)
            p.sy.dma_start(
                out=pview(xnT_out, 128 * (g4 * 4), 512, 0, T),
                in_=xnT_sb[:, (g4 * 4) * T:(g4 * 4 + 4) * T]).inc(stx, 16)
    p.v.tensor_copy(tmp_sb[0:1, 0:8], ab_sb[0:1, 0:8]).inc(ve, 1)
    if io.get("dbg_xn") is not None:
        for g4 in range(8):
            p.sy.dma_start(out=pview(io["dbg_xn"], 128 * (g4 * 4), 512, 0, T),
                           in_=xnT_sb[:, (g4 * 4) * T:(g4 * 4 + 4) * T]).inc(stx, 16)
    cl3()
    cl2()
    cl0()
    cl1()
    return (stx, stx.n), (ve, xn_done[31])



def emit_ffn0(p, xnT_sb, xn_gate, gT_out, io, ldgates=(), ccgate=None):
    p.prefix = "f0_"
    wrt, at, bt = io["wrt"], io["at"], io["bt"]
    br, rep, ones8, onesrow = io["br"], io["rep"], io["ones8"], io["onesrow"]

    ld = p.sem("ld")
    ldg = p.sem("ldg")
    mm = p.sem("mm")
    ve = p.sem("ve")
    sc = p.sem("sc")
    st = p.sem("st")

    wrt_bf = p.sb("wrt", [128, 32 * 8], BF16)
    at_bf = p.sb("at", [128, 32 * 128], BF16)
    bt_bf = p.sb("bt", [128, D], BF16)
    br_sb = p.sb("br", [8, 1], F32)
    rep_sb = p.sb("rep", [8, 128], F32)
    ones8_sb = p.sb("ones8", [8, 1], F32)
    elog = p.sb("elog", [8, T], F32)
    recip = p.sb("recip", [1, T], F32)
    rb_sb = p.sb("rb", [128, T], F32)
    scn = p.sb("scn", [128, T], F32)
    slow = p.sb("slow", [128, T], BF16)
    g_sb = [p.sb(f"g{i}", [128, T], BF16) for i in range(16)]

    ps_log, cl1 = p.scoped_psum("pslog", [8, T], F32)
    ps_a, cl2 = p.scoped_psum("psa", [128, T], F32)
    ps_b, cl3 = p.scoped_psum("psb", [128, T], F32)
    ps_low, cl4 = p.scoped_psum("pslow", [128, T], F32)

    for (sm, val) in ldgates:
        p.sy.wait(sm, val)
        p.g.wait(sm, val)
    gload(p, wrt_bf, wrt, 0, D, 0, 8, ldg)
    gload(p, at_bf, at, 0, D, 0, 128, ldg)
    t3 = p.g.dma_start(out=bt_bf[:], in_=bt).inc(ldg, 16)
    p.sy.dma_start(out=br_sb[:], in_=br).inc(ld, 16)
    p.sy.dma_start(out=rep_sb[:], in_=rep).inc(ld, 16)
    t7 = p.sy.dma_start(out=ones8_sb[:], in_=ones8).inc(ld, 16)

    p.t.wait(ldg, t3)
    p.t.wait(ld, t7)
    p.t.wait(xn_gate[0], xn_gate[1])
    tok = None
    for th in range(2):
        for ct in range(32):
            tok = p.t.matmul(ps_log[:, th * 512:(th + 1) * 512],
                             wrt_bf[:, ct * 8:(ct + 1) * 8],
                             xnT_sb[:, ct * T + th * 512: ct * T + (th + 1) * 512],
                             start=(ct == 0), stop=(ct == 31))
    tok.inc(mm, 1)
    dmy1 = p.t.matmul(ps_a[0:1, 0:512], wrt_bf[:, 0:1], xnT_sb[:, 0:512],
                      start=True, stop=True).inc(mm, 1)
    p.s.wait(mm, dmy1)
    sc_exp = p.s.activation(out=elog[:], in_=ps_log[:], func=AF.Exp, bias=br_sb[:, 0:1]).inc(sc, 1)
    p.t.wait(sc, sc_exp)
    for th in range(2):
        tok = p.t.matmul(ps_log[0:1, th * 512:(th + 1) * 512], ones8_sb[:],
                         elog[:, th * 512:(th + 1) * 512], start=True, stop=True)
    tok.inc(mm, 1)
    for th in range(2):
        tok = p.t.matmul(ps_a[:, th * 512:(th + 1) * 512], rep_sb[:],
                         elog[:, th * 512:(th + 1) * 512], start=True, stop=True)
    mm_sc = tok.inc(mm, 1)
    p.v.wait(mm, mm_sc)
    ve_t = p.v.reciprocal(recip[:], ps_log[0:1, :]).inc(ve, 1)
    p.v.wait(ve, ve_t)
    ve_r16 = p.v.tensor_scalar(out=recip[:], in0=recip[:], scalar1=16.0, scalar2=None,
                               op0=ALU.mult).inc(ve, 1)
    p.t.wait(ve, ve_r16)
    for th in range(2):
        tok = p.t.matmul(ps_b[:, th * 512:(th + 1) * 512], onesrow[:],
                         recip[:, th * 512:(th + 1) * 512], start=True, stop=True)
    mm_rb = tok.inc(mm, 1)
    for th in range(2):
        for ct in range(32):
            tok = p.t.matmul(ps_low[:, th * 512:(th + 1) * 512],
                             at_bf[:, ct * 128:(ct + 1) * 128],
                             xnT_sb[:, ct * T + th * 512: ct * T + (th + 1) * 512],
                             start=(ct == 0), stop=(ct == 31))
    mm_low = tok.inc(mm, 1)
    dmy2 = p.t.matmul(ps_log[0:1, 0:512], wrt_bf[:, 0:1], xnT_sb[:, 0:512],
                      start=True, stop=True).inc(mm, 1)
    p.s.wait(mm, mm_low)
    sc_rb = p.s.copy(rb_sb[:], ps_b[:]).inc(sc, 1)
    p.v.wait(sc, sc_rb)
    ve_scn = p.v.tensor_mul(scn[:], ps_a[:], rb_sb[:]).inc(ve, 1)
    p.v.wait(mm, dmy2)
    p.v.wait(ve, ve_scn)
    ve_slow = p.v.tensor_mul(slow[:], ps_low[:], scn[:]).inc(ve, 1)
    if io.get("dbg") is not None:
        p.sy.wait(sc, sc_exp)
        p.sy.dma_start(out=io["dbg"]["elog"], in_=elog[:]).inc(st, 16)
        p.sy.wait(ve, ve_scn)
        p.sy.dma_start(out=io["dbg"]["scn"], in_=scn[:]).inc(st, 16)
        p.sy.wait(ve, ve_slow)
        p.sy.dma_start(out=io["dbg"]["slow"], in_=slow[:]).inc(st, 16)
    p.t.wait(ve, ve_slow)
    st_tok = [0] * 16
    sc_t = [0] * 33
    for ot in range(32):
        b = ot % 16
        pg = ps_a if ot % 2 == 0 else ps_b
        if ot >= 2:
            p.t.wait(sc, sc_t[ot - 1])
        for th in range(2):
            tok = p.t.matmul(pg[:, th * 512:(th + 1) * 512],
                             bt_bf[:, ot * 128:(ot + 1) * 128],
                             slow[:, th * 512:(th + 1) * 512], start=True, stop=True)
        mm_t = tok.inc(mm, 1)
        p.s.wait(mm, mm_t)
        p.s.wait(st, st_tok[b])
        sc_t[ot] = p.s.activation(out=g_sb[b][:], in_=pg[:], func=AF.Silu).inc(sc, 1)
        if ot >= 1:
            if ot == 1 and ccgate is not None:
                p.sy.wait(ccgate[0], ccgate[1])
            p.sy.wait(sc, sc_t[ot])
            st_tok[(ot - 1) % 16] = p.sy.dma_start(
                out=gT_out[(ot - 1) * 128:ot * 128, :],
                in_=g_sb[(ot - 1) % 16][:]).inc(st, 16)
    sc_t[32] = p.s.copy(rb_sb[0:1, 0:8], scn[0:1, 0:8]).inc(sc, 1)
    p.sy.wait(sc, sc_t[32])
    p.sy.dma_start(out=gT_out[31 * 128:32 * 128, :], in_=g_sb[31 % 16][:]).inc(st, 16)
    cl4()
    cl3()
    cl2()
    cl1()
    return st, st.n


def emit_ffn1(p, xnT_sb, xn_gate, gT_out, io, ldgates=()):
    p.prefix = "f1_"
    qwrep, qzeros, scales = io["qwrep"], io["qzeros"], io["scales"]
    fbias, shiftv, e32 = io["fbias"], io["shiftv"], io["e32"]

    ld = p.sem("ld")
    ldg = p.sem("ldg")
    mm = p.sem("mm")
    ve = p.sem("ve")
    sc = p.sem("sc")
    st = p.sem("st")
    dq = p.sem("dq")

    qz_sb = p.sb("qz", [32, 512], I32)
    zr_sb = p.sb("zr", [32, D], I32)
    sc_sb = p.sb("scales", [32, D], F32)
    negsz = p.sb("negsz", [32, D], F32)
    fbias_sb = p.sb("fb", [128, 32], F32)
    shv_sb = p.sb("shv", [128, 1], I32)
    e32_sb = p.sb("e32", [128, 32 * 32], BF16)
    xnsum = p.sb("xnsum", [32, T], F32)
    qw_sb = [p.sb(f"qw{i}", [128, 512], I32) for i in range(4)]
    sbc_sb = [p.sb(f"sbc{i}", [128, 512], F32) for i in range(4)]
    wnib = [p.sb(f"wn{i}", [128, 512], I32) for i in range(2)]
    w_bf = [p.sb(f"w{i}", [128, 512], BF16) for i in range(2)]
    g_sb = [p.sb(f"g{i}", [128, T], BF16) for i in range(2)]

    ps_cl = []
    ps_g = []
    for i in range(8):
        h, cl = p.scoped_psum(f"psg{i}", [128, 512], F32)
        ps_g.append(h)
        ps_cl.append(cl)

    for (sm, val) in ldgates:
        p.sy.wait(sm, val)
    t1 = p.sy.dma_start(out=qz_sb[:], in_=qzeros).inc(ld, 16)
    t2 = p.sy.dma_start(out=sc_sb[:], in_=scales).inc(ld, 16)
    t3 = p.sy.dma_start(out=fbias_sb[:], in_=fbias).inc(ld, 16)
    t4 = p.sy.dma_start(out=shv_sb[:], in_=shiftv).inc(ld, 16)
    t5 = p.sy.dma_start(out=e32_sb[:], in_=e32).inc(ld, 16)

    p.v.wait(ld, t2)
    ve_t = 0
    for i in range(8):
        zrview = zr_sb[:].rearrange("g (j i) -> g j i", i=8)[:, :, i]
        ve_t = p.v.tensor_scalar(out=zrview, in0=qz_sb[:], scalar1=4 * i, scalar2=15,
                                 op0=ALU.logical_shift_right, op1=ALU.bitwise_and).inc(ve, 1)
    p.v.wait(ve, ve_t)
    ve_negsz = p.v.scalar_tensor_tensor(out=negsz[:], in0=zr_sb[:], scalar=-1.0, in1=sc_sb[:],
                                        op0=ALU.mult, op1=ALU.mult).inc(ve, 1)
    p.t.wait(ld, t5)
    p.t.wait(xn_gate[0], xn_gate[1])
    tok = None
    for th in range(2):
        for t in range(32):
            tok = p.t.matmul(ps_g[th][0:32, :],
                             e32_sb[:, t * 32:(t + 1) * 32],
                             xnT_sb[:, t * T + th * 512: t * T + (th + 1) * 512],
                             start=(t == 0), stop=(t == 31))
    tok.inc(mm, 1)
    dmy = p.t.matmul(ps_g[2][0:1, :], e32_sb[:, 0:1], xnT_sb[:, 0:512],
                     start=True, stop=True).inc(mm, 1)
    p.s.wait(mm, dmy)
    p.s.copy(xnsum[:, 0:512], ps_g[0][0:32, :])
    p.s.copy(xnsum[:, 512:1024], ps_g[1][0:32, :])
    sc_xn = p.s.copy(g_sb[0][0:32, 0:512], xnsum[0:32, 0:512]).inc(sc, 1)

    st_of = {}
    dq_tok = {}
    ve_tok = {}
    og_st = 0
    pending = None
    idx = 0
    for og in range(8):
        of0 = og * 4
        for t in range(32):
            b4 = idx % 4
            b = idx % 2
            if idx >= 2:
                p.sy.wait(dq, dq_tok[idx - 2])
            p.sy.dma_start(out=qw_sb[b4][:],
                           in_=qwrep[128 * t:128 * (t + 1), 512 * og:512 * (og + 1)]).inc(ldg, 16)
            lt2 = p.sy.dma_start(out=sbc_sb[b4][:],
                                 in_=bcast_row(scales, t, 512 * og, 512)).inc(ldg, 16)
            p.v.wait(ldg, lt2)
            if idx == 0:
                p.v.wait(ld, t4)
            if idx >= 2:
                p.v.wait(dq, dq_tok[idx - 2])
            ve_t = p.v.tensor_scalar(out=wnib[b][:], in0=qw_sb[b4][:], scalar1=shv_sb[:],
                                     scalar2=15, op0=ALU.logical_shift_right,
                                     op1=ALU.bitwise_and).inc(ve, 1)
            p.v.wait(ve, ve_t)
            ve_tok[idx] = p.v.tensor_mul(w_bf[b][:], wnib[b][:], sbc_sb[b4][:]).inc(ve, 1)
            p.t.wait(ve, ve_tok[idx])
            if t == 0:
                if og > 0:
                    p.t.wait(st, og_st)
                else:
                    p.t.wait(sc, sc_xn)
            tok = None
            for q in range(4):
                for th in range(2):
                    tok = p.t.matmul(ps_g[q * 2 + th][:],
                                     w_bf[b][:, 128 * q:128 * (q + 1)],
                                     xnT_sb[:, t * T + th * 512: t * T + (th + 1) * 512],
                                     start=(t == 0), stop=False)
            dq_tok[idx] = tok.inc(dq, 1)
            idx += 1
        p.t.wait(ve, ve_negsz)
        for q in range(4):
            for th in range(2):
                tok = p.t.matmul(ps_g[q * 2 + th][:],
                                 negsz[:, 128 * (of0 + q):128 * (of0 + q + 1)],
                                 xnsum[:, th * 512:(th + 1) * 512],
                                 start=False, stop=True)
        mm_t = tok.inc(mm, 1)
        p.s.wait(mm, mm_t)
        if og == 0:
            p.s.wait(ld, t3)
        for q in range(4):
            of = of0 + q
            qb = (og * 4 + q) % 2
            p.s.wait(st, st_of.get(qb, 0))
            sc_t = None
            for th in range(2):
                sc_t = p.s.activation(out=g_sb[qb][:, th * 512:(th + 1) * 512],
                                      in_=ps_g[q * 2 + th][:], func=AF.Silu,
                                      bias=fbias_sb[:, of:of + 1]).inc(sc, 1)
            if pending is not None:
                pof, pqb = pending
                p.sy.wait(sc, sc_t)
                st_of[pqb] = p.sy.dma_start(
                    out=gT_out[pof * 128:(pof + 1) * 128, :],
                    in_=g_sb[pqb][:]).inc(st, 16)
            pending = (of, qb)
        og_st = st.n
    sc_d = p.s.copy(xnsum[0:1, 0:8], fbias_sb[0:1, 0:8]).inc(sc, 1)
    pof, pqb = pending
    p.sy.wait(sc, sc_d)
    p.sy.dma_start(out=gT_out[pof * 128:(pof + 1) * 128, :],
                   in_=g_sb[pqb][:]).inc(st, 16)
    for cl in reversed(ps_cl):
        cl()
    return st, st.n



def emit_qkv(p, layer, xnTf, outs, io, wgates=(), ccgate=None):
    p.prefix = f"q{layer}_"
    p.push_scope()
    wcols = {0: 1536, 1: 768}[layer]
    nob = {0: 12, 1: 6}[layer]
    w_in = io["w"]
    cq, sq, ck, sk = io["cq"], io["sq"], io["ck"], io["sk"]

    ld = p.sem("ld")
    ldw = p.sem("ldw")
    xl = p.sem("xl")
    xl2 = p.sem("xl2")
    mm = p.sem("mm")
    ve = p.sem("ve")
    sc = p.sem("sc")
    st = p.sem("st")
    po = p.sem("po")

    w_sb = p.sb("w", [128, 32 * wcols], BF16)
    xn_sb = [p.sb(f"xn{i}", [128, 32 * 512], BF16) for i in range(2)]
    rope_sb = [p.sb(f"rope{i}", [128, 512], F32) for i in range(4)]
    ev_sb = [p.sb(f"ev{i}", [128, 512], F32) for i in range(2)]
    esw_sb = [p.sb(f"esw{i}", [128, 512], F32) for i in range(2)]
    t1_sb = [p.sb(f"t1{i}", [128, 512], F32) for i in range(2)]
    t2_sb = [p.sb(f"t2{i}", [128, 512], F32) for i in range(2)]
    out_sb = [p.sb(f"out{i}", [128, 512], BF16) for i in range(4)]
    if layer == 1:
        bq_sb = p.sb("bq", [128, 6], F32)
    ps_cl = []
    ps = []
    for i in range(6):
        h, cl = p.scoped_psum(f"ps{i}", [128, 512], F32)
        ps.append(h)
        ps_cl.append(cl)
    ps_junk, cl_junk = p.scoped_psum("psjunk", [1, 512], F32)
    ps_cl.append(cl_junk)
    nglobal = 16 * nob

    for (sm, val) in wgates:
        p.g.wait(sm, val)
        p.sy.wait(sm, val)
    if ccgate is not None:
        p.sy.wait(ccgate[0], ccgate[1])
    wt = gload(p, w_sb, w_in, 0, D, 0, wcols, ldw)
    if layer == 1:
        tb = p.sy.dma_start(out=bq_sb[:], in_=io["bqkv"]).inc(ld, 16)
    p.t.wait(ldw, wt)
    if layer == 1:
        p.t.wait(ld, tb)

    if layer == 0:
        kinds = ["q"] * 4 + ["k"] * 4 + ["v"] * 4
        head_of_ob = [0, 1, 2, 3, 0, 1, 2, 3, 0, 1, 2, 3]
    else:
        kinds = ["q"] * 4 + ["k", "v"]
        head_of_ob = [0, 1, 2, 3, 0, 0]

    mm_end = [0] * 16
    sc_tok = {}
    st_tok = [0] * 4
    ve_tok = {}
    po_last = 0
    ve4_tok = [0, 0]
    rt_toks = [0] * 16
    gidx = 0
    xt_tok = [0] * 16
    xt2_tok = [0] * 16

    def load_xn(tc):
        xb = tc % 2
        r, th = tc // 2, tc % 2
        if tc < 2:
            if ccgate is not None:
                p.sy.wait(ccgate[0], ccgate[1])
                p.s.wait(ccgate[0], ccgate[1])
        else:
            p.sy.wait(mm, mm_end[tc - 2])
            p.s.wait(mm, mm_end[tc - 2])
        xt_tok[tc] = gload(p, xn_sb[xb], xnTf, r * D, D // 2, th * 512, 512, xl,
                           eng=p.sy)
        xt2_tok[tc] = gload(p, xn_sb[xb], xnTf, r * D + D // 2, D // 2, th * 512,
                            512, xl2, dst_c0=16 * 512, eng=p.s)

    load_xn(0)
    load_xn(1)
    for tc in range(16):
        xb = tc % 2
        if tc >= 1:
            p.sy.wait(ve, ve_tok.get(tc - 1, 0))
        rt_t = None
        for i, tabap in enumerate((cq, sq, ck, sk)):
            p.sy.dma_start(out=rope_sb[i][0:64, :],
                           in_=tabap[:, tc * 512:(tc + 1) * 512]).inc(ld, 16)
            rt_t = p.sy.dma_start(out=rope_sb[i][64:128, :],
                                  in_=tabap[:, tc * 512:(tc + 1) * 512]).inc(ld, 16)
        rt_toks[tc] = rt_t
        p.t.wait(xl, xt_tok[tc])
        p.t.wait(xl2, xt2_tok[tc])
        for ob in range(nob):
            pb = ps[ob % 6]
            if gidx >= 5:
                p.t.wait(sc, sc_tok[gidx - 5])
            tok = None
            for ct in range(32):
                tok = p.t.matmul(pb[:],
                                 w_sb[:, ct * wcols + ob * 128: ct * wcols + (ob + 1) * 128],
                                 xn_sb[xb][:, ct * 512:(ct + 1) * 512],
                                 start=(ct == 0), stop=(ct == 31))
            mm_t = tok.inc(mm, 1)
            chase = mm_t + 1 if gidx < nglobal - 1 else nglobal + 1
            kind = kinds[ob]
            h = head_of_ob[ob]
            obuf = ob % 4 if layer == 0 else (ob if ob < 4 else ob - 2)
            obuf = obuf % 4
            if kind == "v":
                ob_buf = out_sb[obuf]
                p.s.wait(mm, chase)
                p.s.wait(st, st_tok[obuf])
                if layer == 1:
                    sc_t = p.s.add(ob_buf[:], pb[:], bq_sb[:, 5:6]).inc(sc, 1)
                else:
                    sc_t = p.s.copy(ob_buf[:], pb[:]).inc(sc, 1)
                sc_tok[gidx] = sc_t
                p.sy.wait(sc, sc_t + 1)
                st_tok[obuf] = p.sy.dma_start(
                    out=outs["vT"][h * 128:(h + 1) * 128, tc * 512:(tc + 1) * 512],
                    in_=ob_buf[:]).inc(st, 16)
            else:
                e2 = ob % 2
                eb = ev_sb[e2]
                p.s.wait(mm, chase)
                p.s.wait(ve, ve4_tok[e2])
                if layer == 1:
                    bias = bq_sb[:, ob:ob + 1] if kind == "q" else bq_sb[:, 4:5]
                    sc_t = p.s.add(eb[:], pb[:], bias).inc(sc, 1)
                else:
                    sc_t = p.s.copy(eb[:], pb[:]).inc(sc, 1)
                sc_tok[gidx] = sc_t
                c_t = rope_sb[0] if kind == "q" else rope_sb[2]
                s_t = rope_sb[1] if kind == "q" else rope_sb[3]
                tb1 = t1_sb[e2]
                tb2 = t2_sb[e2]
                esw = esw_sb[e2]
                ob_buf = out_sb[obuf]
                p.g.wait(sc, sc_t)
                p.g.wait(ve, ve4_tok[e2])
                p.g.tensor_copy(esw[0:64, :], eb[64:128, :])
                gsw = p.g.tensor_copy(esw[64:128, :], eb[0:64, :]).inc(po, 1)
                p.v.wait(sc, sc_t)
                p.v.wait(po, gsw)
                p.v.wait(ld, rt_toks[tc])
                p.v.wait(st, st_tok[obuf])
                p.v.tensor_mul(tb1[:], eb[:], c_t[:])
                p.v.tensor_mul(tb2[:], esw[:], s_t[:]).inc(ve, 1)
                p.v.tensor_sub(ob_buf[0:64, :], tb1[0:64, :], tb2[0:64, :])
                p.v.tensor_add(ob_buf[64:128, :], tb1[64:128, :], tb2[64:128, :]).inc(ve, 1)
                ve_4 = p.v.tensor_copy(tb1[0:1, 0:8], tb2[0:1, 0:8]).inc(ve, 1)
                ve4_tok[e2] = ve_4
                ve_tok[tc] = ve_4
                p.sy.wait(ve, ve_4)
                dst = outs["qT"] if kind == "q" else outs["kT"]
                st_tok[obuf] = p.sy.dma_start(
                    out=dst[h * 128:(h + 1) * 128, tc * 512:(tc + 1) * 512],
                    in_=ob_buf[:]).inc(st, 16)
            gidx += 1
        mm_end[tc] = mm.n
        if tc + 2 < 16:
            load_xn(tc + 2)
    p.t.matmul(ps_junk[0:1, :], w_sb[:, 0:1], xn_sb[1][:, 0:512],
               start=True, stop=True).inc(mm, 1)
    p.v.tensor_copy(xn_sb[0][0:1, 0:8], rope_sb[0][0:1, 0:8]).inc(ve, 1)
    p.s.copy(xn_sb[0][0:1, 8:16], rope_sb[0][0:1, 0:8]).inc(sc, 1)
    for cl in reversed(ps_cl):
        cl()
    p.pop_scope()
    return st, st.n



def attn_pieces(layer, qc):
    out = []
    q_lo_t, q_hi_t = 4 * qc, 4 * qc + 3
    if layer == 0:
        for kt in range(0, q_hi_t + 1):
            col_lo = max(0, 128 * (kt - q_lo_t))
            diag = col_lo if kt >= q_lo_t else None
            out.append((kt, "full", col_lo, 512, diag))
    else:
        for kt in range(0, q_hi_t + 1):
            lo_t = max(q_lo_t, kt)
            hi_t = min(q_hi_t, kt + 7)
            if lo_t <= hi_t:
                col_lo = 128 * (lo_t - q_lo_t)
                col_hi = 128 * (hi_t - q_lo_t) + 128
                diag = 128 * (kt - q_lo_t) if kt >= q_lo_t else None
                out.append((kt, "full", col_lo, col_hi, diag))
            if kt % 4 == 3:
                lo_t = max(q_lo_t, kt + 8)
                if lo_t <= q_hi_t:
                    out.append((kt, "upper", 128 * (lo_t - q_lo_t), 512, None))
            else:
                i = kt + 8
                if q_lo_t <= i <= q_hi_t:
                    c0 = 128 * (i - q_lo_t)
                    out.append((kt, "upper", c0, c0 + 64, None))
    return out


def emit_attn(p, layer, io, gates=()):
    p.prefix = f"a{layer}_"
    p.push_scope()
    qT, kT, vT, ctxa = io["qT"], io["kT"], io["vT"], io["ctxa"]
    lrowd = p.dram(f"a{layer}_lrowd", [2, 512], F32)

    ld = p.sem("ld")
    ldh = p.sem("ldh")
    mm = p.sem("mm")
    ve = p.sem("ve")
    sc = p.sem("sc")
    st = p.sem("st")
    pv = p.sem("pv")
    rbs = p.sem("rbs")

    ident_sb, tri_sb = io["ident_sb"], io["tri_sb"]
    onesc_sb, zeroc_sb = io["onescol_sb"], io["zerob_sb"]
    onesr_sb = io["onesrowf_sb"]
    csh_sb = io["csh0_sb"] if layer == 0 else io["csh1_sb"]
    cld = io["constld"]

    qh = [p.sb(f"q{i}", [128, S], BF16) for i in range(2)]
    kh = [p.sb(f"k{i}", [128, S], BF16) for i in range(2)]
    vh = [p.sb(f"v{i}", [128, S], BF16) for i in range(2)]
    vtm = [p.sb(f"vtm{i}", [128, S], BF16) for i in range(2)]
    pt = [p.sb(f"pt{i}", [128, 512], BF16) for i in range(4)]
    lrow = [p.sb(f"lrow{i}", [1, 512], F32) for i in range(2)]
    rb_sb = [p.sb(f"rb{i}", [128, 512], F32) for i in range(2)]
    ctx_sb = p.sb("ctx", [128, S], BF16)

    ps_cl = []

    def spsum(name, shape, dt):
        h, cl = p.scoped_psum(name, shape, dt)
        ps_cl.append(cl)
        return h

    ps_s = [spsum(f"pss{i}", [128, 512], F32) for i in range(4)]
    ps_o = [spsum(f"pso{i}", [128, 512], F32) for i in range(2)]
    ps_l = spsum("psl", [33, 512], F32)
    ps_tr = spsum("pstr", [128, 512], BF16)

    ldh_tok = [0] * 4
    pv_end = [0] * 4
    vtm_ve = [0] * 4
    st_end = [0] * 4
    exp_tok = {}
    pvp_tok = {}
    veo_tok = {}
    scrb_tok = {}
    pieces_sbi = 0

    for (sm, val) in gates:
        p.g.wait(sm, val)

    for j in range(4):
        hb = j % 2
        kv = j if layer == 0 else 0
        if j >= 2:
            p.g.wait(pv, pv_end[j - 2])
        l1 = p.g.dma_start(out=qh[hb][:], in_=qT[128 * j:128 * (j + 1), :]).inc(ldh, 16)
        l2 = p.g.dma_start(out=kh[hb][:], in_=kT[128 * kv:128 * (kv + 1), :]).inc(ldh, 16)
        l3 = p.g.dma_start(out=vh[hb][:], in_=vT[128 * kv:128 * (kv + 1), :]).inc(ldh, 16)
        ldh_tok[j] = l3
        p.t.wait(ldh, l3)
        if j == 0:
            p.t.wait(cld[0], cld[1])
        if j >= 2:
            p.t.wait(pv, pv_end[j - 2])
        ev_pr = 0
        for grp in range(16):
            p.t.wait(ve, ev_pr)
            tok = None
            for q in range(4):
                ktile = grp * 4 + q
                tok = p.t.transpose(ps_tr[:, 128 * q:128 * (q + 1)],
                                    vh[hb][:, 128 * ktile:128 * (ktile + 1)], ident_sb[:])
            mm_t = tok.inc(mm, 1)
            p.v.wait(mm, mm_t)
            ev_pr = p.v.tensor_copy(vtm[hb][:, 512 * grp:512 * (grp + 1)], ps_tr[:]).inc(ve, 1)
        vtm_ve[j] = ev_pr
        p.t.wait(ve, ev_pr)
        p.t.wait(ldh, l1)

        for qc in range(16):
            pp = qc % 2
            pieces = attn_pieces(layer, qc)
            if (j, qc) != (0, 0):
                prev = veo_tok.get((j, qc - 2)) if qc >= 2 else veo_tok.get((j - 1, 14 + qc))
                if prev:
                    p.t.wait(ve, prev + 1)
            p.t.matmul(ps_o[pp][:, 0:512], zeroc_sb[:], kh[hb][:, 0:512],
                       start=True, stop=False)
            p.t.matmul(ps_l[32 * pp:32 * pp + 1, 0:512], zeroc_sb[:, 0:1],
                       kh[hb][:, 0:512], start=True, stop=False)
            for (kt, kpart, c0, c1_, diag) in pieces:
                sb = pieces_sbi % 4
                pieces_sbi += 1
                if kpart == "full":
                    lhs = kh[hb][:, 128 * kt:128 * (kt + 1)]
                    prow = slice(0, 128)
                else:
                    lhs = kh[hb][:, 128 * kt + 64:128 * (kt + 1)]
                    prow = slice(64, 128)
                if pieces_sbi > 3:
                    p.t.wait(sc, exp_tok[pieces_sbi - 1 - 3])
                s_t = p.t.matmul(ps_s[sb][prow, c0:c1_], lhs,
                                 qh[hb][:, 512 * qc + c0:512 * qc + c1_],
                                 start=True, stop=True).inc(mm, 1)
                p.s.wait(mm, s_t)
                if pieces_sbi > 3:
                    p.s.wait(pv, pvp_tok[pieces_sbi - 1 - 3])
                e_t = p.s.activation(out=pt[sb][prow, c0:c1_], in_=ps_s[sb][prow, c0:c1_],
                                     func=AF.Exp, bias=csh_sb[prow, 0:1]).inc(sc, 1)
                exp_tok[pieces_sbi - 1] = e_t
                m_t = None
                if diag is not None:
                    p.v.wait(sc, e_t)
                    m_t = p.v.tensor_mul(pt[sb][:, diag:diag + 128],
                                         pt[sb][:, diag:diag + 128], tri_sb[:]).inc(ve, 1)
                if m_t is not None:
                    p.t.wait(ve, m_t)
                else:
                    p.t.wait(sc, e_t)
                vlhs = vtm[hb][prow, 128 * kt:128 * (kt + 1)]
                p.t.matmul(ps_o[pp][:, c0:c1_], vlhs, pt[sb][prow, c0:c1_],
                           start=False, stop=False)
                pvp = p.t.matmul(ps_l[32 * pp:32 * pp + 1, c0:c1_], onesc_sb[prow, :],
                                 pt[sb][prow, c0:c1_],
                                 start=False, stop=False).inc(pv, 1)
                pvp_tok[pieces_sbi - 1] = pvp
            mm_all = pv.n
            p.v.wait(pv, mm_all)
            p.v.reciprocal(lrow[pp][:], ps_l[32 * pp:32 * pp + 1, :]).inc(ve, 1)
            ve_l = p.v.tensor_copy(rb_sb[pp][0:1, 0:4],
                                   lrow[pp][0:1, 0:4]).inc(ve, 1)
            prev_ctx = veo_tok.get((j, qc - 2)) if qc >= 2 else veo_tok.get((j - 1, 14 + qc))
            if prev_ctx:
                p.sy.wait(ve, prev_ctx + 1)
            p.sy.wait(ve, ve_l)
            p.sy.dma_start(out=AP(lrowd.tensor, lrowd.offset + 512 * pp,
                                  [[512, 1], [1, 512]]),
                           in_=lrow[pp][:]).inc(rbs, 16)
            rb_t = p.sy.dma_start(out=rb_sb[pp][:],
                                  in_=bcast_row(lrowd, pp, 0, 512)).inc(rbs, 16)
            p.v.wait(rbs, rb_t)
            if qc == 0 and j > 0:
                p.v.wait(st, st_end[j - 1])
            veo_tok[(j, qc)] = p.v.tensor_mul(ctx_sb[:, 512 * qc:512 * (qc + 1)],
                                              ps_o[pp][:], rb_sb[pp][:]).inc(ve, 1)
        pv_end[j] = pv.n
        if j == 3:
            p.v.tensor_copy(lrow[0][0:1, 0:8], rb_sb[0][0:1, 0:8]).inc(ve, 1)
        p.sy.wait(ve, veo_tok[(j, 15)] + 1)
        for r in range(NCORE):
            p.sy.dma_start(out=ctxa[512 * r + 128 * j: 512 * r + 128 * (j + 1), :],
                           in_=ctx_sb[:, T * r:T * (r + 1)]).inc(st, 16)
        st_end[j] = st.n
    p.sy.wait(st, st.n)
    for cl in reversed(ps_cl):
        cl()
    p.pop_scope()
    return st, st.n



def emit_oproj_ring(p, layer, wot, qst_gate, n_ungated=3):
    p.prefix = f"o{layer}_"
    bufs, closers = [], []
    for i in range(3):
        h, cl = p.scoped_sb(f"wo{i}", [128, D], BF16)
        bufs.append(h)
        closers.append(cl)
    ring = {
        "bufs": bufs,
        "closers": closers,
        "wl": p.sem("wl"),
        "op_mm": p.sem("op_mm"),
        "wot": wot,
        "qst_gate": qst_gate,
        "n_ungated": n_ungated,
    }
    return ring


def emit_ring_loads(p, layer, ring, first, last):
    p.prefix = f"o{layer}_"
    for ot in range(first, last):
        if ot < ring["n_ungated"]:
            sm, val = ring["qst_gate"]
            p.g.wait(sm, val)
        else:
            p.g.wait(ring["op_mm"], ot - 2)
        p.g.dma_start(out=ring["bufs"][ot % 3][:],
                      in_=rowsap(ring["wot"], 128 * ot)).inc(ring["wl"], 16)


def emit_oproj(p, layer, io, ring, ccgate):
    p.prefix = f"o{layer}_"
    p.push_scope()
    ctx_sl, obias = io["ctx_sl"], io["obias"]
    gT, xT_in, xoT = io["gT"], io["x"], io["xo"]

    ld = p.sem("ld")
    ld2 = p.sem("ld2")
    gx = p.sem("gx")
    mm = ring["op_mm"]
    ve = p.sem("ve")
    sto = p.sem("sto")

    ctx_sb = p.sb("ctx", [128, 32 * T], BF16)
    ob_sb = p.sb("ob", [128, 32], F32)
    g_sb = [p.sb(f"g{i}", [128, T], BF16) for i in range(3)]
    x_sb = [p.sb(f"x{i}", [128, T], F32) for i in range(3)]
    tmp_sb = [p.sb(f"tmp{i}", [128, T], F32) for i in range(2)]
    xo_sb = [p.sb(f"xo{i}", [128, T], F32) for i in range(2)]

    ps_cl = []
    ps_s = []
    for i in range(3):
        h, cl = p.scoped_psum(f"pss{i}", [128, T], F32)
        ps_s.append(h)
        ps_cl.append(cl)
    ps_junk, cl_junk = p.scoped_psum("psjunk", [1, 512], F32)
    ps_cl.append(cl_junk)

    wl = ring["wl"]

    p.sy.wait(ccgate[0], ccgate[1])
    p.s.wait(ccgate[0], ccgate[1])
    t0 = t0b = None
    for i, ct0 in enumerate(range(0, 32, 4)):
        if i % 2 == 0:
            t0 = p.sy.dma_start(out=ctx_sb[:, ct0 * T:(ct0 + 4) * T],
                                in_=pview(ctx_sl, 128 * ct0, 512, 0, T)).inc(ld, 16)
        else:
            t0b = p.s.dma_start(out=ctx_sb[:, ct0 * T:(ct0 + 4) * T],
                                in_=pview(ctx_sl, 128 * ct0, 512, 0, T)).inc(ld2, 16)
    t2 = p.sy.dma_start(out=ob_sb[:], in_=obias).inc(ld, 16)

    gstg = io.get("g_gate")
    gx_tok = [0] * 3
    ve_tok = [0] * 32
    st_tok = [0, 0]

    def load_gx(ot):
        b = ot % 3
        if ot < 3 and gstg is not None:
            p.sy.wait(gstg[0], gstg[1])
        gx_tok[b] = p.sy.dma_start(out=g_sb[b][:], in_=rowsap(gT, 128 * ot)).inc(gx, 16)
        gx_tok[b] = p.sy.dma_start(out=x_sb[b][:], in_=rowsap(xT_in, 128 * ot)).inc(gx, 16)

    load_gx(0)
    load_gx(1)
    load_gx(2)
    for ot in range(32):
        b = ot % 3
        b2 = ot % 2
        p.t.wait(wl, 16 * (ot + 1))
        if ot == 0:
            p.t.wait(ld, t0)
            p.t.wait(ld2, t0b)
        if ot >= 3:
            p.t.wait(ve, ve_tok[ot - 2])
        tok = None
        for ct in range(32):
            for th in range(2):
                tok = p.t.matmul(ps_s[b][:, 512 * th:512 * (th + 1)],
                                 ring["bufs"][ot % 3][:, ct * 128:(ct + 1) * 128],
                                 ctx_sb[:, ct * T + 512 * th: ct * T + 512 * (th + 1)],
                                 start=(ct == 0), stop=(ct == 31))
        mm_t = tok.inc(mm, 1)
        p.v.wait(mm, mm_t + 1 if ot < 31 else 33)
        p.v.wait(gx, gx_tok[b])
        if ot == 0:
            p.v.wait(ld, t2)
        p.v.wait(sto, st_tok[b2])
        p.v.scalar_tensor_tensor(out=tmp_sb[b2][:], in0=ps_s[b][:],
                                 scalar=ob_sb[:, ot:ot + 1], in1=g_sb[b][:],
                                 op0=ALU.add, op1=ALU.mult)
        ve_tok[ot] = p.v.tensor_add(xo_sb[b2][:], tmp_sb[b2][:], x_sb[b][:]).inc(ve, 1)
        if ot == 31:
            p.v.tensor_copy(tmp_sb[0][0:1, 0:8], ob_sb[0:1, 0:8]).inc(ve, 1)
        p.sy.wait(ve, ve_tok[ot] + 1)
        st_tok[b2] = p.sy.dma_start(out=rowsap(xoT, 128 * ot), in_=xo_sb[b2][:]).inc(sto, 16)
        if ot + 3 < 32:
            load_gx(ot + 3)
    p.t.matmul(ps_junk[0:1, :], ring["bufs"][31 % 3][:, 0:1], ctx_sb[:, 0:512],
               start=True, stop=True).inc(mm, 1)
    p.sy.wait(sto, sto.n)
    for cl in reversed(ps_cl):
        cl()
    p.pop_scope()
    return sto, sto.n



def _rope_tables(theta, scale):
    j = np.arange(64, dtype=np.float64)
    inv = theta ** (-2.0 * j / 128.0)
    t = np.arange(S, dtype=np.float64)
    f = np.outer(inv, t)
    return (np.cos(f) * scale).astype(np.float32), (np.sin(f) * scale).astype(np.float32)


def host_prep(inputs):
    d = {}
    d["ident"] = np.eye(128, dtype=np.float32).astype(BF)
    kk, qq = np.meshgrid(np.arange(128), np.arange(128), indexing="ij")
    d["tri"] = (kk <= qq).astype(np.float32).astype(BF)
    d["zerob"] = np.zeros((128, 128), dtype=np.float32).astype(BF)
    d["onescol"] = np.ones((128, 1), dtype=np.float32).astype(BF)
    d["onescolf"] = np.ones((128, 1), dtype=np.float32)
    d["onesrow"] = np.ones((1, 128), dtype=np.float32)
    d["ones8"] = np.ones((8, 1), dtype=np.float32)
    rep = np.zeros((8, 128), dtype=np.float32)
    for h in range(8):
        rep[h, 16 * h:16 * h + 16] = 1.0
    d["rep"] = rep
    d["shiftv"] = (4 * (np.arange(128) % 8)).astype(np.int32).reshape(128, 1)
    e32 = np.zeros((128, 32 * 32), dtype=np.float32)
    for t in range(32):
        e32[:, 32 * t + t] = 1.0
    d["e32"] = e32.astype(BF)
    d["csh0"] = np.full((128, 1), -C_SHIFT[0], dtype=np.float32)
    d["csh1"] = np.full((128, 1), -C_SHIFT[1], dtype=np.float32)

    cq0, sq0 = _rope_tables(10000.0, 1.0 / math.sqrt(HD))
    ck0, sk0 = _rope_tables(10000.0, 1.0)
    d["rope0"] = (cq0, sq0, ck0, sk0)
    cq1, sq1 = _rope_tables(1000000.0, 1.0 / HD)
    ck1, sk1 = _rope_tables(1000000.0, 1.0)
    d["rope1"] = (cq1, sq1, ck1, sk1)

    perm = np.concatenate([np.arange(0, 128, 2), np.arange(1, 128, 2)])
    wq = np.asarray(inputs["L0_attn_wq"])
    wk = np.asarray(inputs["L0_attn_wk"])
    wv = np.asarray(inputs["L0_attn_wv"])
    d["w0"] = []
    for c in range(NCORE):
        cols = []
        for w, permute in ((wq, True), (wk, True), (wv, False)):
            for j in range(4):
                h = 4 * c + j
                idx = h * 128 + (perm if permute else np.arange(128))
                cols.append(w[:, idx])
        d["w0"].append(np.ascontiguousarray(np.concatenate(cols, axis=1), dtype=np.float32))

    wqkv = np.asarray(inputs["L1_attn_w_qkv"])
    bqkv = np.asarray(inputs["L1_attn_b_qkv"])
    d["w1"] = []
    d["b1"] = []
    for c in range(NCORE):
        rows = np.arange(768 * c, 768 * (c + 1))
        d["w1"].append(np.ascontiguousarray(wqkv[rows].T, dtype=np.float32))
        d["b1"].append(np.ascontiguousarray(bqkv[rows].reshape(6, 128).T, dtype=np.float32))

    d["wrt"] = np.ascontiguousarray(np.asarray(inputs["L0_ffn_Wr"]).T, dtype=np.float32)
    d["at"] = np.ascontiguousarray(np.asarray(inputs["L0_ffn_A"]).reshape(128, D).T,
                                   dtype=np.float32)
    d["bt"] = np.ascontiguousarray(
        np.asarray(inputs["L0_ffn_B"]).transpose(0, 2, 1).reshape(128, D), dtype=np.float32)
    d["br"] = np.asarray(inputs["L0_ffn_br"], dtype=np.float32).reshape(8, 1)

    qw = np.asarray(inputs["L1_ffn_qweight"])
    d["qwrep"] = np.ascontiguousarray(np.repeat(qw, 8, axis=0))
    d["qzeros"] = np.asarray(inputs["L1_ffn_qzeros"])
    d["scales"] = np.asarray(inputs["L1_ffn_scales"], dtype=np.float32)
    d["fbias"] = np.ascontiguousarray(
        np.asarray(inputs["L1_ffn_bias"], dtype=np.float32).reshape(32, 128).T)

    wo0 = np.asarray(inputs["L0_attn_wo"], dtype=np.float32)
    wo1 = np.ascontiguousarray(np.asarray(inputs["L1_attn_w_dense"]).T, dtype=np.float32)
    d["wo0t"] = np.ascontiguousarray(
        wo0.reshape(32, 128, 32, 128).transpose(2, 1, 0, 3).reshape(D, D))
    d["wo1t"] = np.ascontiguousarray(
        wo1.reshape(32, 128, 32, 128).transpose(2, 1, 0, 3).reshape(D, D))
    d["ob0"] = np.zeros((128, 32), dtype=np.float32)
    d["ob1"] = np.ascontiguousarray(
        np.asarray(inputs["L1_attn_b_dense"], dtype=np.float32).reshape(32, 128).T)

    xT = np.asarray(inputs["L0_attn_x"], dtype=np.float32).reshape(S, D).T
    d["x"] = [np.ascontiguousarray(xT[:, T * c:T * (c + 1)]) for c in range(NCORE)]
    return d


CONST_KEYS = ("ident", "tri", "zerob", "onescol", "onescolf", "onesrow", "ones8", "rep",
              "shiftv", "e32", "csh0", "csh1")



def build_fused():
    p = P("mega")
    x_in = p.din("x", [D, T], F32)
    w0 = p.din("w0", [D, 1536], F32)
    w1 = p.din("w1", [D, 768], F32)
    bqkv1 = p.din("bqkv1", [128, 6], F32)
    wo0t = p.din("wo0t", [D, D], F32)
    wo1t = p.din("wo1t", [D, D], F32)
    ob0 = p.din("ob0", [128, 32], F32)
    ob1 = p.din("ob1", [128, 32], F32)
    wrt = p.din("wrt", [D, 8], F32)
    at = p.din("at", [D, 128], F32)
    bt = p.din("bt", [128, D], F32)
    br = p.din("br", [8, 1], F32)
    qwrep = p.din("qwrep", [D, D], I32)
    qzeros = p.din("qzeros", [32, 512], I32)
    scales = p.din("scales", [32, D], F32)
    fbias = p.din("fbias", [128, 32], F32)
    ropes = {}
    for l in (0, 1):
        for nm in ("cq", "sq", "ck", "sk"):
            ropes[(l, nm)] = p.din(f"{nm}{l}", [64, S], F32)
    xo_out = p.dout("xo", [D, T], F32)

    p.prefix = "c_"
    cled = p.sem("cled")
    io_c = {}
    for nm, shape, dt in (("ident", [128, 128], BF16), ("tri", [128, 128], BF16),
                          ("zerob", [128, 128], BF16), ("onescol", [128, 1], BF16),
                          ("onescolf", [128, 1], F32), ("onesrowf", [1, 128], F32),
                          ("csh0", [128, 1], F32), ("csh1", [128, 1], F32)):
        dnm = "onesrow" if nm == "onesrowf" else nm
        din = p.din(dnm, shape, dt)
        sbuf = p.sb(nm, shape, dt)
        p.sy.dma_start(out=sbuf[:], in_=din).inc(cled, 16)
        io_c[nm + "_sb"] = sbuf
        io_c[nm + "_din"] = din
    cld = (cled, cled.n)
    io_c["constld"] = cld

    ones8 = p.din("ones8", [8, 1], F32)
    rep = p.din("rep", [8, 128], F32)
    shiftv = p.din("shiftv", [128, 1], I32)
    e32 = p.din("e32", [128, 32 * 32], BF16)

    def mk(name, shape, dt):
        return p.dout(name, shape, dt) if name in TAPS else p.dram(name, shape, dt)

    xnT = [mk(f"i_xnT{l}", [D, T], BF16) for l in (0, 1)]
    xnTf = [p.dram(f"i_xnTf{l}", [NCORE * D, T], BF16) for l in (0, 1)]
    gT = [mk(f"i_gT{l}", [D, T], BF16) for l in (0, 1)]
    qTd = [mk(f"i_qT{l}", [512, S], BF16) for l in (0, 1)]
    kTd = [mk("i_kT0", [512, S], BF16), mk("i_kT1", [128, S], BF16)]
    vTd = [mk("i_vT0", [512, S], BF16), mk("i_vT1", [128, S], BF16)]
    ctxa = [mk(f"i_ctxa{l}", [NCORE * 512, T], BF16) for l in (0, 1)]
    ctxsl = [p.dram(f"i_ctxsl{l}", [D, T], BF16) for l in (0, 1)]
    x1d = mk("i_x1", [D, T], F32)

    cc = [p.sem(f"cc{i}") for i in range(4)]

    prev_gate = ()
    for l in (0, 1):
        xin_l = x_in if l == 0 else x1d
        p.push_scope()
        xnT_sb = p.sb(f"xnTsb{l}", [128, 32 * T], BF16)
        lnio = {"onescolf": io_c["onescolf_sb"], "onesrow": io_c["onesrowf_sb"],
                "constld": cld}
        if f"dbg_xn{l}" in TAPS:
            lnio["dbg_xn"] = p.dout(f"dbg_xn{l}", [D, T], BF16)
        (stx, stx_n), xn_gate = emit_ln(p, l, xin_l, xnT_sb, xnT[l], lnio,
                                        gates=prev_gate)
        if l == 0:
            f_io = {"wrt": wrt, "at": at, "bt": bt, "br": br, "rep": rep,
                    "ones8": ones8, "onesrow": io_c["onesrowf_sb"]}
            if "dbg_ffn0" in TAPS:
                f_io["dbg"] = {"elog": p.dout("dbg_elog", [8, T], F32),
                               "scn": p.dout("dbg_scn", [128, T], F32),
                               "slow": p.dout("dbg_slow", [128, T], BF16)}
            stg, stg_n = emit_ffn0(p, xnT_sb, xn_gate, gT[l], f_io, ldgates=prev_gate,
                                   ccgate=(cc[2 * l], 1))
        else:
            f_io = {"qwrep": qwrep, "qzeros": qzeros, "scales": scales,
                    "fbias": fbias, "shiftv": shiftv, "e32": e32}
            stg, stg_n = emit_ffn1(p, xnT_sb, xn_gate, gT[l], f_io, ldgates=prev_gate)
        p.prefix = f"c{l}_"
        p.g.wait(stx, stx_n)
        if l == 1:
            p.g.wait(stg, stg_n)
        p.g.collective_compute(
            "AllGather", ALU.bypass, replica_groups=[list(range(NCORE))],
            ins=[xnT[l]],
            outs=[xnTf[l]],
        ).inc(cc[2 * l], 1)
        p.pop_scope()

        qio = {"w": w0 if l == 0 else w1}
        for nm in ("cq", "sq", "ck", "sk"):
            qio[nm] = ropes[(l, nm)]
        if l == 1:
            qio["bqkv"] = bqkv1
        outs = {"qT": qTd[l], "kT": kTd[l], "vT": vTd[l]}
        qst, qst_n = emit_qkv(p, l, xnTf[l], outs, qio,
                              wgates=((stx, stx_n), (stg, stg_n)),
                              ccgate=(cc[2 * l], 1))

        ring = emit_oproj_ring(p, l, wo0t if l == 0 else wo1t, (qst, qst_n))

        aio = dict(io_c)
        aio.update({"qT": qTd[l], "kT": kTd[l], "vT": vTd[l], "ctxa": ctxa[l]})
        ast, ast_n = emit_attn(p, l, aio, gates=((qst, qst_n),))

        emit_ring_loads(p, l, ring, 0, 3)
        p.prefix = f"d{l}_"
        p.g.wait(ast, ast_n)
        p.g.collective_compute(
            "AllToAll", ALU.bypass, replica_groups=[list(range(NCORE))],
            ins=[ctxa[l]], outs=[ctxsl[l]],
        ).inc(cc[2 * l + 1], 1)
        emit_ring_loads(p, l, ring, 3, 32)

        oio = {"ctx_sl": ctxsl[l], "obias": ob0 if l == 0 else ob1, "gT": gT[l],
               "x": xin_l, "xo": x1d if l == 0 else xo_out,
               "g_gate": (stg, stg_n)}
        ost, ost_n = emit_oproj(p, l, oio, ring, ccgate=(cc[2 * l + 1], 1))
        for cl in reversed(ring["closers"]):
            cl()
        prev_gate = ((ost, ost_n),)
    return p.emit()


def kernel(**inputs):
    d = host_prep(inputs)
    cores = list(range(NCORE))
    with allow_ici_all_to_all():
        nc = build_fused()
        maps = []
        for c in cores:
            m = {k: d[k] for k in CONST_KEYS}
            m.update({"x": d["x"][c], "w0": d["w0"][c], "w1": d["w1"][c],
                      "bqkv1": d["b1"][c], "wo0t": d["wo0t"], "wo1t": d["wo1t"],
                      "ob0": d["ob0"], "ob1": d["ob1"], "wrt": d["wrt"], "at": d["at"],
                      "bt": d["bt"], "br": d["br"], "qwrep": d["qwrep"],
                      "qzeros": d["qzeros"], "scales": d["scales"], "fbias": d["fbias"]})
            for l in (0, 1):
                cq, sq, ck, sk = d[f"rope{l}"]
                m.update({f"cq{l}": cq, f"sq{l}": sq, f"ck{l}": ck, f"sk{l}": sk})
            maps.append(m)
        res = run_bass_kernel_spmd(nc, maps, core_ids=cores)
    y = np.concatenate([np.asarray(res.results[c]["xo"]) for c in cores], axis=1)
    return np.ascontiguousarray(y.T.reshape(1, S, D), dtype=np.float32)
